# version 13
# baseline (speedup 1.0000x reference)
import numpy as np
import concourse.bass as bass
from concourse.bass_core_rust import IndirectOffsetOnAxis
import concourse.mybir as mybir
from concourse.bass_utils import run_bass_kernel_spmd

F32 = mybir.dt.float32
F16 = mybir.dt.float16
AF = mybir.ActivationFunctionType
ALU = mybir.AluOpType
AX = mybir.AxisListType

EPS = 1e-5


class Dims:
    def __init__(self, T=8192, D=1024, KD=256, M=64, V=256, TBLK=512):
        self.T = T
        self.D = D
        self.KD = KD
        self.M = M
        self.V = V
        self.C = 128
        self.TBLK = TBLK
        self.NTB = T // TBLK
        self.NCH = T // self.C
        self.DC = D // 128
        self.JCOLS = 3 * KD + M
        self.NSL = 4
        self.SL = T // 4


DIMS = Dims()


ENGINES = ("sync", "scalar", "tensor", "vector", "gpsimd")


class _Op:
    __slots__ = ("engine", "build", "deps", "is_dma", "chan", "needs_inc", "seq", "tag", "no_inc")

    def __init__(self, engine, build, is_dma, chan, tag, no_inc=False):
        self.engine = engine
        self.build = build
        self.deps = []
        self.is_dma = is_dma
        self.chan = chan
        self.needs_inc = False
        self.seq = None
        self.tag = tag
        self.no_inc = no_inc


class Prog:
    def __init__(self, nc):
        self.nc = nc
        self.ops = []
        self.last_write = {}
        self.readers = {}

    def emit(self, engine, build, reads=(), writes=(), is_dma=False, tag="", dma_chan=None,
             no_inc=False):
        if is_dma:
            chan = dma_chan or ("dma_" + (writes[0] if writes else reads[0]))
        else:
            chan = engine
        op = _Op(engine, build, is_dma, chan, tag, no_inc)
        deps = []

        def need(other):
            return not (other.is_dma and is_dma and other.chan == chan)

        for b in reads:
            w = self.last_write.get(b)
            if w is not None and need(w):
                deps.append(w)
        for b in writes:
            w = self.last_write.get(b)
            if w is not None and need(w):
                deps.append(w)
            for r in self.readers.get(b, ()):
                if need(r):
                    deps.append(r)
        op.deps = deps
        for b in reads:
            self.readers.setdefault(b, []).append(op)
        for b in writes:
            self.last_write[b] = op
            self.readers[b] = []
        self.ops.append(op)
        return op

    def dma(self, engine, build, reads=(), writes=(), tag="", dma_chan=None):
        return self.emit(engine, build, reads, writes, is_dma=True, tag=tag, dma_chan=dma_chan)

    def finalize_and_run(self, stack, block):
        nc = self.nc
        for op in self.ops:
            for d in op.deps:
                if not d.no_inc:
                    d.needs_inc = True
        for op in self.ops:
            if op.is_dma:
                op.needs_inc = True
        counts = {}
        for op in self.ops:
            if op.needs_inc:
                amt = 16 if op.is_dma else 1
                counts[op.chan] = counts.get(op.chan, 0) + amt
                op.seq = counts[op.chan]
        chans = {c: stack.enter_context(nc.semaphore(("ch_" + c)[:48])) for c in counts}
        dma_finals = {c: v for c, v in counts.items() if c.startswith("dma_")}

        def mk(engine_name):
            def body(eng):
                for op in self.ops:
                    if op.engine != engine_name:
                        continue
                    w = {}
                    for d in op.deps:
                        if d.seq is None:
                            continue
                        w[d.chan] = max(w.get(d.chan, 0), d.seq)
                    for c, v in w.items():
                        eng.wait_ge(chans[c], v)
                    inst = op.build(eng)
                    if op.needs_inc and not op.no_inc:
                        inst.then_inc(chans[op.chan], 16 if op.is_dma else 1)
                if engine_name == "sync":
                    for c, v in dma_finals.items():
                        eng.wait_ge(chans[c], v)
            return body

        block.sync(mk("sync"))
        block.scalar(mk("scalar"))
        block.tensor(mk("tensor"))
        block.vector(mk("vector"))
        block.gpsimd(mk("gpsimd"))


class Rec:

    def __init__(self):
        self.items = []

    def emit(self, *a, **kw):
        self.items.append(("emit", a, kw))

    def dma(self, *a, **kw):
        self.items.append(("dma", a, kw))


def replay_interleaved(P, recs):
    recs = [r for r in recs if r.items]
    k = len(recs)
    idx = [0] * k
    while True:
        best, bf = -1, 2.0
        for i in range(k):
            if idx[i] < len(recs[i].items):
                f = idx[i] / len(recs[i].items)
                if f < bf:
                    bf, best = f, i
        if best < 0:
            break
        kind, a, kw = recs[best].items[idx[best]]
        idx[best] += 1
        (P.emit if kind == "emit" else P.dma)(*a, **kw)


def emit_consts(P, t):
    P.emit("gpsimd", lambda e: e.memset(t["ident"][:, :], 0.0), writes=["ident"])
    P.emit("gpsimd", lambda e: e.affine_select(out=t["ident"][:, :], in_=t["ident"][:, :],
                                               compare_op=ALU.not_equal, fill=1.0, base=0,
                                               pattern=[[-1, 128]], channel_multiplier=1),
           reads=["ident"], writes=["ident"])
    P.emit("gpsimd", lambda e: e.memset(t["cb_one"][:, :], 1.0), writes=["cb_one"])
    P.emit("gpsimd", lambda e: e.memset(t["cb_eps"][:, :], EPS), writes=["cb_eps"])


def make_ln_scratch(sb, tag, n=2):
    out = []
    for p in range(n):
        d = {}
        d["bno"] = sb(f"bno{tag}{p}", [128, 12])
        d["bna"] = sb(f"bna{tag}{p}", [128, 2])
        d["lnv_c"] = sb(f"lnvc{tag}{p}", [128, 1])
        d["r_c"] = sb(f"rc{tag}{p}", [128, 1])
        d["nmr_c"] = sb(f"nmrc{tag}{p}", [128, 1])
        out.append(d)
    return out


def emit_ln_tile(P, t, lnp, tag, par, xt_name, xt_ap, xn_name, xn_ap, dm):
    s = lnp[par]
    n = {k: f"ln{tag}{par}_{k}" for k in s}
    bno, bna = s["bno"], s["bna"]
    P.emit("vector", lambda e: e.bn_stats(bno[:, 0:6], xt_ap[:, 0:512]),
           reads=[xt_name], writes=[n["bno"]])
    P.emit("vector", lambda e: e.bn_stats(bno[:, 6:12], xt_ap[:, 512:1024]),
           reads=[xt_name], writes=[n["bno"]])
    P.emit("vector", lambda e: e.bn_aggr(bna[:, :], bno[:, :]), reads=[n["bno"]], writes=[n["bna"]])
    P.emit("scalar", lambda e: e.activation(out=s["lnv_c"][:, :], in_=bna[:, 1:2], func=AF.Ln,
                                            bias=t["cb_eps"][:, 0:1]),
           reads=[n["bna"], "cb_eps"], writes=[n["lnv_c"]])
    P.emit("scalar", lambda e: e.activation(out=s["r_c"][:, :], in_=s["lnv_c"][:, :], func=AF.Exp,
                                            scale=-0.5),
           reads=[n["lnv_c"]], writes=[n["r_c"]])
    P.emit("vector", lambda e: e.scalar_tensor_tensor(out=s["nmr_c"][:, :], in0=bna[:, 0:1],
                                                      scalar=-1.0, in1=s["r_c"][:, :],
                                                      op0=ALU.mult, op1=ALU.mult),
           reads=[n["bna"], n["r_c"]], writes=[n["nmr_c"]])
    P.emit("scalar", lambda e: e.activation(out=xn_ap, in_=xt_ap, func=AF.Identity,
                                            scale=s["r_c"][:, 0:1], bias=s["nmr_c"][:, 0:1]),
           reads=[xt_name, n["r_c"], n["nmr_c"]], writes=[xn_name])


def build_fused(dm):
    from contextlib import ExitStack
    nc = bass.Bass("TRN2", target_bir_lowering=False)
    stack = ExitStack()
    T, D, KD, M, V, C = dm.T, dm.D, dm.KD, dm.M, dm.V, dm.C
    SL = dm.SL
    NC5 = SL // 128

    xb = nc.dram_tensor("xb", [T, D], F32, kind="ExternalInput").ap()
    wcat = nc.dram_tensor("wcat", [D, dm.JCOLS], F32, kind="ExternalInput").ap()
    wr = nc.dram_tensor("wr", [1, D], F32, kind="ExternalInput").ap()
    gnwd = nc.dram_tensor("gnwd", [1, V], F32, kind="ExternalInput").ap()
    wbd = nc.dram_tensor("wbd", [D, D], F32, kind="ExternalInput").ap()
    wod = nc.dram_tensor("wod", [D, D], F32, kind="ExternalInput").ap()
    xs = nc.dram_tensor("xs", [SL, D], F32, kind="ExternalInput").ap()
    agidx = nc.dram_tensor("agidx", [128, 4 * NC5], mybir.dt.int32, kind="ExternalInput").ap()
    y = nc.dram_tensor("y", [SL, D], F32, kind="ExternalOutput").ap()
    agin = nc.dram_tensor("agin", [T, V], F32, kind="Internal").ap()
    agout = nc.dram_tensor("agout", [4 * T, V], F32, kind="Internal").ap()

    sb = lambda name, shape, dt=F32: stack.enter_context(nc.sbuf_tensor(name, shape, dt))
    thr128 = sb("thr128", [128, 1])
    lo_t = sb("lo_t", [1, 1]); hi_t = sb("hi_t", [1, 1]); mid_t = sb("mid_t", [1, 1])
    sel_t = sb("sel_t", [1, 1]); isel_t = sb("isel_t", [1, 1])
    d1_t = sb("d1_t", [1, 1]); d2_t = sb("d2_t", [1, 1]); tot_t = sb("tot_t", [1, 1])
    mid128 = sb("mid128", [128, 1]); cnt_c = sb("cnt_c", [128, 1])
    onescol = sb("onescol", [128, 1]); dead64 = sb("dead64", [128, T // 128])
    fencet = sb("fencet", [1, 1])
    t = {}
    t["ident"] = sb("ident", [128, 128])
    t["cb_one"] = sb("cb_one", [128, 1])
    t["cb_eps"] = sb("cb_eps", [128, 1])
    mask01 = sb("mask01", [128, 128])
    ones128 = sb("ones128", [1, 128])
    zcol = sb("zcol", [M, 1])
    gnw = sb("gnw", [128, V])
    scores = sb("scores", [128, T // 128])
    stack1 = ExitStack()
    sb1 = lambda name, shape, dt=F32: stack1.enter_context(nc.sbuf_tensor(name, shape, dt))
    wr128 = sb1("wr128", [128, D])
    NXT = 2
    xt = [sb1(f"xt{i}", [128, D]) for i in range(NXT)]
    xn = [sb1(f"xn{i}", [128, D]) for i in range(NXT)]
    lnp = make_ln_scratch(sb1, "a", 2)
    dead_v = [sb1("dead_v0", [128, D])] * 2
    WIN = 2 * dm.TBLK
    xnTh = sb1("xnTh", [128, dm.DC * WIN], F16)
    xnTl = sb1("xnTl", [128, dm.DC * WIN], F16)
    xnhi = [sb1(f"xnhi{i}", [128, D], F16) for i in range(2)]
    xnlo = [sb1(f"xnlo{i}", [128, D], F16) for i in range(2)]
    wch = [sb1(f"wch{dc}", [128, dm.JCOLS], F16) for dc in range(dm.DC)]
    wcl = [sb1(f"wcl{dc}", [128, dm.JCOLS], F16) for dc in range(dm.DC)]
    QW = min(1024, T)
    qf = [sb1(f"qf{i}", [128, 2 * QW]) for i in range(2)]
    kf = [sb1(f"kf{i}", [128, 2 * QW]) for i in range(2)]
    vf = [sb1(f"vf{i}", [128, 2 * QW]) for i in range(2)]
    useg = sb1("useg", [M, dm.TBLK])
    Lseg = sb1("Lseg", [M, dm.TBLK])
    Bg = sb1("Bg", [M, T])
    hkr = [[sb1(f"hk{r}_0", [128, M]), sb1(f"hk{r}_1", [128, M])] for r in range(3)]
    hvr = [sb1(f"hv{r}", [M, V]) for r in range(3)]

    def mk_cs(p):
        d = {}
        for nm, shape in (("bias_c", [M, 1]), ("en", [M, C]), ("stT", [M, C]), ("brel", [M, C]),
                          ("eend", [M, 1]), ("eer", [1, M]), ("eer128", [128, M]), ("E", [128, M]),
                          ("stil", [128, M]), ("atm", [128, 128]), ("logits", [128, M]),
                          ("rmax", [128, 1]), ("nb16", [128, 1]), ("p_sb", [128, M]),
                          ("den", [128, 1]), ("rden", [128, 1]), ("pe", [128, M]),
                          ("pet", [M, 128]), ("gtm", [128, 128]), ("vt", [128, V]),
                          ("kt", [128, KD]), ("o32", [128, V]), ("osq", [128, V]),
                          ("ssq", [128, 1]), ("lnv", [128, 1]), ("rstd", [128, 1]),
                          ("on", [128, V]), ("t1a", [128, M]), ("t1b", [128, M]), ("t2", [M, V])):
            d[nm] = sb1(f"cs{p}_{nm}", shape)
        return d
    cs = [mk_cs(0), mk_cs(1)]

    pb = [stack.enter_context(nc.psum_tensor(f"pb{i}", [128, 512], F32)) for i in range(8)]
    ps_proj = pb[0]
    psT = (pb[0], pb[0])

    def mk_ps(base):
        bsm, bvk, bop = base
        return {
            "at": pb[bsm][:, 0:128], "gt": pb[bsm][:, 128:256],
            "logit": pb[bsm][:, 256:320], "brelT": pb[bsm][:, 320:384],
            "eer": pb[bsm][0:1, 384:448], "eerep": pb[bsm][:, 448:512],
            "stil": pb[bop][:, 384:448],
            "vt": pb[bvk][:, 0:256], "kt": pb[bvk][:, 256:512],
            "o": pb[bop][:, 0:256], "pet": pb[bop][0:M, 256:384],
            "bsm": f"b{bsm}", "bvk": f"b{bvk}", "bop": f"b{bop}",
        }
    psc = [mk_ps((1, 2, 3)), mk_ps((4, 5, 6))]
    ps_hk = pb[7][:, 0:128]
    ps_hv = pb[7][0:M, 128:384]

    P = Prog(nc)

    emit_consts(P, t)
    P.emit("gpsimd", lambda e: e.memset(mask01[:, :], 1.0), writes=["mask01"])
    P.emit("gpsimd", lambda e: e.affine_select(out=mask01[:, :], in_=mask01[:, :], compare_op=ALU.is_ge,
                                               fill=0.0, base=0, pattern=[[1, 128]], channel_multiplier=-1),
           reads=["mask01"], writes=["mask01"])
    P.emit("gpsimd", lambda e: e.memset(ones128[:, :], 1.0), writes=["ones128"])
    P.emit("gpsimd", lambda e: e.memset(zcol[:, :], 0.0), writes=["zcol"])
    P.emit("gpsimd", lambda e: e.memset(hkr[2][0][:, :], 0.0), writes=["hk2_0"])
    P.emit("gpsimd", lambda e: e.memset(hkr[2][1][:, :], 0.0), writes=["hk2_1"])
    P.emit("gpsimd", lambda e: e.memset(hvr[2][:, :], 0.0), writes=["hv2"])
    P.dma("sync", lambda e: e.dma_start(out=gnw[:, :], in_=gnwd.to_broadcast([128, V])), writes=["gnw"])
    P.dma("sync", lambda e: e.dma_start(out=wr128[:, :], in_=wr.to_broadcast([128, D])), writes=["wr128"])
    wcs = dead_v[0][:, 0:dm.JCOLS]
    for dc in range(dm.DC):
        P.dma("sync", lambda e, dc=dc: e.dma_start(out=wcs, in_=wcat[dc * 128:(dc + 1) * 128, :]),
              writes=["dead_v0"])
        P.emit("vector", lambda e, dc=dc: e.tensor_copy(wch[dc][:, :], wcs),
               reads=["dead_v0"], writes=[f"wch{dc}"])
        P.emit("vector", lambda e, dc=dc: e.tensor_sub(wcl[dc][:, :], wcs, wch[dc][:, :]),
               reads=["dead_v0", f"wch{dc}"], writes=[f"wcl{dc}"])

    def chunk_env(c):
        par = c % 2
        s = cs[par]
        ps = psc[par]
        n = {k: f"c{par}_{k}" for k in s}
        qwin = (c * C // QW) % 2
        return par, s, ps, n, qwin

    def emit_chunk_head(c, PP):
        c0 = c * C
        par, s, ps, n, qwin = chunk_env(c)
        bsm, bvk, bop = ps["bsm"], ps["bvk"], ps["bop"]
        off = (c * C) % QW
        Bc = Bg[:, c0 - 1:c0] if c > 0 else zcol[:, :]
        bsrc = ["Bg"] if c > 0 else ["zcol"]
        qfp, kfp, vfp = qf[qwin], kf[qwin], vf[qwin]
        qn, kn, vn = f"qf{qwin}", f"kf{qwin}", f"vf{qwin}"
        hki = hkr[(c - 1) % 3]
        hkin = [f"hk{(c - 1) % 3}_0", f"hk{(c - 1) % 3}_1"]
        PP.emit("vector", lambda e: e.tensor_scalar_mul(s["bias_c"][:, :], Bc, -1.0 / 8.0),
               reads=bsrc, writes=[n["bias_c"]])
        PP.emit("scalar", lambda e: e.activation(out=s["en"][:, :], in_=Bg[:, c0:c0 + C], func=AF.Exp,
                                                scale=1.0 / 8.0, bias=s["bias_c"][:, 0:1]),
               reads=["Bg", n["bias_c"]], writes=[n["en"]])
        PP.emit("vector", lambda e: e.tensor_scalar_add(s["stT"][:, 0:1], s["en"][:, 0:1], -1.0),
               reads=[n["en"]], writes=[n["stT"]])
        PP.emit("vector", lambda e: e.tensor_sub(s["stT"][:, 1:C], s["en"][:, 1:C], s["en"][:, 0:C - 1]),
               reads=[n["en"]], writes=[n["stT"]])
        PP.emit("vector", lambda e: e.tensor_scalar(out=s["brel"][:, :], in0=Bg[:, c0:c0 + C],
                                                   scalar1=Bc, scalar2=None, op0=ALU.subtract),
               reads=["Bg"] + bsrc, writes=[n["brel"]])
        PP.emit("scalar", lambda e: e.activation(out=s["eend"][:, :], in_=s["brel"][:, C - 1:C],
                                                func=AF.Exp, scale=-1.0 / 8.0),
               reads=[n["brel"]], writes=[n["eend"]])
        PP.emit("tensor", lambda e: e.transpose(ps["brelT"], s["brel"][:, :], t["ident"][0:M, 0:M]),
               reads=[n["brel"], "ident"], writes=[bsm])
        PP.emit("scalar", lambda e: e.activation(out=s["E"][:, :], in_=ps["brelT"], func=AF.Exp,
                                                scale=-1.0 / 8.0),
               reads=[bsm], writes=[n["E"]])
        PP.emit("tensor", lambda e: e.transpose(ps["eer"], s["eend"][:, :], t["ident"][0:M, 0:M]),
               reads=[n["eend"], "ident"], writes=[bsm])
        PP.emit("vector", lambda e: e.tensor_copy(s["eer"][:, :], ps["eer"]), reads=[bsm], writes=[n["eer"]])
        PP.emit("tensor", lambda e: e.matmul(ps["eerep"], lhsT=ones128[:, :], rhs=s["eer"][:, :],
                                            start=True, stop=True),
               reads=["ones128", n["eer"]], writes=[bsm])
        PP.emit("vector", lambda e: e.tensor_copy(s["eer128"][:, :], ps["eerep"]), reads=[bsm],
               writes=[n["eer128"]])
        PP.emit("tensor", lambda e: e.transpose(ps["stil"], s["stT"][:, :], t["ident"][0:M, 0:M]),
               reads=[n["stT"], "ident"], writes=[bop])
        PP.emit("vector", lambda e: e.tensor_copy(s["stil"][:, :], ps["stil"]), reads=[bop],
               writes=[n["stil"]])
        PP.emit("tensor", lambda e: e.matmul(ps["at"], lhsT=kfp[:, off:off + C], rhs=qfp[:, off:off + C],
                                            start=True, stop=False),
               reads=[kn, qn], writes=[bsm])
        PP.emit("tensor", lambda e: e.matmul(ps["at"], lhsT=kfp[:, QW + off:QW + off + C],
                                            rhs=qfp[:, QW + off:QW + off + C], start=False, stop=True),
               reads=[kn, qn], writes=[bsm])
        PP.emit("vector", lambda e: e.scalar_tensor_tensor(out=s["atm"][:, :], in0=ps["at"], scalar=1.0,
                                                          in1=mask01[:, :], op0=ALU.bypass, op1=ALU.mult),
               reads=[bsm, "mask01"], writes=[n["atm"]])
        PP.emit("tensor", lambda e: e.matmul(ps["logit"], lhsT=s["atm"][:, :], rhs=s["stil"][:, :],
                                            start=True, stop=False),
               reads=[n["atm"], n["stil"]], writes=[bsm])
        PP.emit("tensor", lambda e: e.matmul(ps["logit"], lhsT=qfp[:, off:off + C], rhs=hki[0][:, :],
                                            start=False, stop=False),
               reads=[qn, hkin[0]], writes=[bsm])
        PP.emit("tensor", lambda e: e.matmul(ps["logit"], lhsT=qfp[:, QW + off:QW + off + C],
                                            rhs=hki[1][:, :], start=False, stop=True),
               reads=[qn, hkin[1]], writes=[bsm])
        PP.emit("tensor", lambda e: e.transpose(ps["vt"][:, 0:128], vfp[:, off:off + C], t["ident"][:, :]),
               reads=[vn, "ident"], writes=[bvk])
        PP.emit("tensor", lambda e: e.transpose(ps["vt"][:, 128:256], vfp[:, QW + off:QW + off + C],
                                               t["ident"][:, :]),
               reads=[vn, "ident"], writes=[bvk])
        PP.emit("scalar", lambda e: e.copy(s["vt"][:, :], ps["vt"]), reads=[bvk], writes=[n["vt"]])
        PP.emit("tensor", lambda e: e.transpose(ps["kt"][:, 0:128], kfp[:, off:off + C], t["ident"][:, :]),
               reads=[kn, "ident"], writes=[bvk])
        PP.emit("tensor", lambda e: e.transpose(ps["kt"][:, 128:256], kfp[:, QW + off:QW + off + C],
                                               t["ident"][:, :]),
               reads=[kn, "ident"], writes=[bvk])
        PP.emit("scalar", lambda e: e.copy(s["kt"][:, :], ps["kt"]), reads=[bvk], writes=[n["kt"]])

    def emit_chunk_state(c, PP):
        par, s, ps, n, qwin = chunk_env(c)
        hki = hkr[(c - 1) % 3]
        hkin = [f"hk{(c - 1) % 3}_0", f"hk{(c - 1) % 3}_1"]
        hko = hkr[c % 3]
        hkon = [f"hk{c % 3}_0", f"hk{c % 3}_1"]
        hvi, hvin = hvr[(c - 1) % 3], f"hv{(c - 1) % 3}"
        hvo, hvon = hvr[c % 3], f"hv{c % 3}"
        PP.emit("tensor", lambda e: e.matmul(ps_hk[:, 0:64], lhsT=s["kt"][:, 0:128], rhs=s["stil"][:, :],
                                            start=True, stop=True),
               reads=[n["kt"], n["stil"]], writes=["b7"])
        PP.emit("tensor", lambda e: e.matmul(ps_hk[:, 64:128], lhsT=s["kt"][:, 128:256],
                                            rhs=s["stil"][:, :], start=True, stop=True),
               reads=[n["kt"], n["stil"]], writes=["b7"])
        for i, tn in ((0, "t1a"), (1, "t1b")):
            PP.emit("vector", lambda e, i=i, tn=tn: e.tensor_add(s[tn][:, :],
                                                                ps_hk[:, 64 * i:64 * (i + 1)],
                                                                hki[i][:, :]),
                   reads=["b7", hkin[i]], writes=[n[tn]])
            PP.emit("vector", lambda e, i=i, tn=tn: e.tensor_mul(hko[i][:, :], s[tn][:, :],
                                                                s["eer128"][:, :]),
                   reads=[n[tn], n["eer128"]], writes=[hkon[i]])
        PP.emit("tensor", lambda e: e.matmul(ps_hv, lhsT=s["stil"][:, :], rhs=s["vt"][:, :],
                                            start=True, stop=True),
               reads=[n["stil"], n["vt"]], writes=["b7"])
        PP.emit("vector", lambda e: e.tensor_add(s["t2"][:, :], ps_hv, hvi[:, :]),
               reads=["b7", hvin], writes=[n["t2"]])
        PP.emit("vector", lambda e: e.tensor_scalar(out=hvo[:, :], in0=s["t2"][:, :],
                                                   scalar1=s["eend"][:, 0:1], scalar2=None, op0=ALU.mult),
               reads=[n["t2"], n["eend"]], writes=[hvon])

    def emit_chunk_tail(c, PP):
        c0 = c * C
        par, s, ps, n, qwin = chunk_env(c)
        bsm, bvk, bop = ps["bsm"], ps["bvk"], ps["bop"]
        hvi, hvin = hvr[(c - 1) % 3], f"hv{(c - 1) % 3}"
        PP.emit("vector", lambda e: e.tensor_tensor(out=s["logits"][:, :], in0=ps["logit"],
                                                   in1=s["E"][:, :], op=ALU.mult),
               reads=[bsm, n["E"]], writes=[n["logits"]])
        PP.emit("vector", lambda e: e.reduce_max(s["rmax"][:, :], s["logits"][:, :], axis=AX.X),
               reads=[n["logits"]], writes=[n["rmax"]])
        PP.emit("vector", lambda e: e.tensor_scalar_mul(s["nb16"][:, :], s["rmax"][:, :], -1.0 / 16.0),
               reads=[n["rmax"]], writes=[n["nb16"]])
        PP.emit("scalar", lambda e: e.activation(out=s["p_sb"][:, :], in_=s["logits"][:, :], func=AF.Exp,
                                                scale=1.0 / 16.0, bias=s["nb16"][:, 0:1],
                                                accum_out=s["den"][:, :]),
               reads=[n["logits"], n["nb16"]], writes=[n["p_sb"], n["den"]])
        PP.emit("vector", lambda e: e.reciprocal(s["rden"][:, :], s["den"][:, :]),
               reads=[n["den"]], writes=[n["rden"]])
        PP.emit("vector", lambda e: e.scalar_tensor_tensor(out=s["pe"][:, :], in0=s["p_sb"][:, :],
                                                          scalar=s["rden"][:, 0:1], in1=s["E"][:, :],
                                                          op0=ALU.mult, op1=ALU.mult),
               reads=[n["p_sb"], n["rden"], n["E"]], writes=[n["pe"]])
        PP.emit("tensor", lambda e: e.transpose(ps["pet"], s["pe"][:, :], t["ident"][:, :]),
               reads=[n["pe"], "ident"], writes=[bop])
        PP.emit("vector", lambda e: e.tensor_copy(s["pet"][:, :], ps["pet"]), reads=[bop], writes=[n["pet"]])
        PP.emit("tensor", lambda e: e.matmul(ps["gt"], lhsT=s["stT"][:, :], rhs=s["pet"][:, :],
                                            start=True, stop=True),
               reads=[n["stT"], n["pet"]], writes=[bsm])
        PP.emit("vector", lambda e: e.scalar_tensor_tensor(out=s["gtm"][:, :], in0=ps["gt"], scalar=1.0,
                                                          in1=mask01[:, :], op0=ALU.bypass, op1=ALU.mult),
               reads=[bsm, "mask01"], writes=[n["gtm"]])
        PP.emit("tensor", lambda e: e.matmul(ps["o"], lhsT=s["gtm"][:, :], rhs=s["vt"][:, :],
                                            start=True, stop=False),
               reads=[n["gtm"], n["vt"]], writes=[bop])
        PP.emit("tensor", lambda e: e.matmul(ps["o"], lhsT=s["pet"][:, :], rhs=hvi[:, :],
                                            start=False, stop=True),
               reads=[n["pet"], hvin], writes=[bop])
        PP.emit("vector", lambda e: e.tensor_copy(s["o32"][:, :], ps["o"]), reads=[bop], writes=[n["o32"]])
        PP.emit("scalar", lambda e: e.activation(out=s["osq"][:, :], in_=s["o32"][:, :], func=AF.Square,
                                                accum_out=s["ssq"][:, :]),
               reads=[n["o32"]], writes=[n["osq"], n["ssq"]])
        PP.emit("scalar", lambda e: e.activation(out=s["lnv"][:, :], in_=s["ssq"][:, :], func=AF.Ln,
                                                scale=1.0 / V, bias=t["cb_eps"][:, 0:1]),
               reads=[n["ssq"], "cb_eps"], writes=[n["lnv"]])
        PP.emit("scalar", lambda e: e.activation(out=s["rstd"][:, :], in_=s["lnv"][:, :], func=AF.Exp,
                                                scale=-0.5),
               reads=[n["lnv"]], writes=[n["rstd"]])
        PP.emit("vector", lambda e: e.scalar_tensor_tensor(out=s["on"][:, :], in0=s["o32"][:, :],
                                                          scalar=s["rstd"][:, 0:1], in1=gnw[:, :],
                                                          op0=ALU.mult, op1=ALU.mult),
               reads=[n["o32"], n["rstd"], "gnw"], writes=[n["on"]])
        PP.dma("sync", lambda e: e.dma_start(out=agin[c0:c0 + C, :], in_=s["on"][:, :]),
               reads=[n["on"]], writes=[f"agin{par}"], dma_chan=f"dma_oout{par}")

    TPB = dm.TBLK // 128

    def emit_tblock(tb, PP):
        parity = tb % 2
        woff = parity * dm.TBLK
        for j in range(TPB):
            i = tb * TPB + j
            xtb, xnb = xt[i % NXT], xn[i % NXT]
            xtn, xnn = f"xt{i % NXT}", f"xn{i % NXT}"
            PP.dma("sync", lambda e, i=i, xtb=xtb: e.dma_start(out=xtb[:, :],
                                                              in_=xb[i * 128:(i + 1) * 128, :]),
                  writes=[xtn])
            emit_ln_tile(PP, t, lnp, "a", i % 2, xtn, xtb[:, :], xnn, xnb[:, :], dm)
            PP.emit("vector", lambda e, xnb=xnb, i=i: e.scalar_tensor_tensor(
                out=dead_v[i % 2][:, :], in0=xnb[:, :], scalar=1.0, in1=wr128[:, :],
                op0=ALU.bypass, op1=ALU.mult, accum_out=scores[:, i:i + 1]),
                reads=[xnn, "wr128"], writes=[f"dead_v{i % 2}", "scores"])
            tcol = woff + j * 128
            hb, lb = xnhi[i % 2], xnlo[i % 2]
            hn, ln_ = f"xnhi{i % 2}", f"xnlo{i % 2}"
            PP.emit("vector", lambda e, hb=hb, xnb=xnb: e.tensor_copy(hb[:, :], xnb[:, :]),
                    reads=[xnn], writes=[hn])
            PP.emit("vector", lambda e, hb=hb, lb=lb, xnb=xnb: e.tensor_sub(lb[:, :], xnb[:, :],
                                                                            hb[:, :]),
                    reads=[xnn, hn], writes=[ln_])
            for dc in range(dm.DC):
                PP.dma("sync", lambda e, dc=dc, hb=hb, tcol=tcol: e.dma_start_transpose(
                    out=xnTh[:, dc * WIN + tcol:dc * WIN + tcol + 128],
                    in_=hb[:, dc * 128:(dc + 1) * 128]),
                    reads=[hn], writes=["xnTh"], dma_chan=f"dma_xnTh{i % 2}")
                PP.dma("sync", lambda e, dc=dc, lb=lb, tcol=tcol: e.dma_start_transpose(
                    out=xnTl[:, dc * WIN + tcol:dc * WIN + tcol + 128],
                    in_=lb[:, dc * 128:(dc + 1) * 128]),
                    reads=[ln_], writes=["xnTl"], dma_chan=f"dma_xnTl{i % 2}")
        jblocks = [(0, 128, "qf", 0), (128, 256, "qf", 1), (256, 384, "kf", 0), (384, 512, "kf", 1),
                   (512, 640, "vf", 0), (640, 768, "vf", 1), (768, 832, "z", 0)]
        qwin = (tb * dm.TBLK // QW) % 2
        qoff = (tb * dm.TBLK) % QW
        for jbi, (j0, j1, kind, half) in enumerate(jblocks):
            pjb = ps_proj
            pjn = "b0"
            passes = [(wch, "wch", xnTh, "xnTh"), (wcl, "wcl", xnTh, "xnTh"),
                      (wch, "wch", xnTl, "xnTl")]
            for pi, (wtl, wtn, xv, xn_) in enumerate(passes):
                for dc in range(dm.DC):
                    PP.emit("tensor", lambda e, dc=dc, j0=j0, j1=j1, woff=woff, pjb=pjb,
                            wtl=wtl, xv=xv, pi=pi: e.matmul(
                        pjb[0:(j1 - j0), 0:dm.TBLK], lhsT=wtl[dc][:, j0:j1],
                        rhs=xv[:, dc * WIN + woff:dc * WIN + woff + dm.TBLK],
                        start=(pi == 0 and dc == 0), stop=(pi == 2 and dc == dm.DC - 1)),
                        reads=[f"{wtn}{dc}", xn_], writes=[pjn])
            if kind == "z":
                PP.emit("scalar", lambda e, pjb=pjb: e.activation(out=useg[:, 0:dm.TBLK],
                                                                 in_=pjb[0:M, 0:dm.TBLK],
                                                                 func=AF.Exp, scale=-1.0),
                       reads=[pjn], writes=["useg"])
                PP.emit("scalar", lambda e: e.activation(out=Lseg[:, 0:dm.TBLK], in_=useg[:, 0:dm.TBLK],
                                                        func=AF.Ln, bias=t["cb_one"][0:M, 0:1]),
                       reads=["useg", "cb_one"], writes=["Lseg"])
                ini = 0.0 if tb == 0 else Bg[:, tb * dm.TBLK - 1:tb * dm.TBLK]
                PP.emit("vector", lambda e, tb=tb, ini=ini: e.tensor_tensor_scan(
                    out=Bg[:, tb * dm.TBLK:(tb + 1) * dm.TBLK], data0=Lseg[:, 0:dm.TBLK],
                    data1=Lseg[:, 0:dm.TBLK], initial=ini, op0=ALU.add, op1=ALU.bypass),
                    reads=["Lseg", "Bg"], writes=["Bg"])
            else:
                dst = {"qf": qf, "kf": kf, "vf": vf}[kind][qwin]
                dn = f"{kind}{qwin}"
                PP.emit("scalar", lambda e, dst=dst, half=half, qoff=qoff, pjb=pjb: e.copy(
                    dst[:, half * QW + qoff:half * QW + qoff + dm.TBLK], pjb[:, 0:dm.TBLK]),
                       reads=[pjn], writes=[dn])

    TB_PER_W = QW // dm.TBLK
    NW = dm.T // QW

    def emit_p1_window(w, PP):
        for tbl in range(TB_PER_W):
            emit_tblock(w * TB_PER_W + tbl, PP)
        qwin = w % 2
        for buf, nm in ((qf[qwin], f"qf{qwin}"), (kf[qwin], f"kf{qwin}")):
            PP.emit("scalar", lambda e, buf=buf: e.activation(out=buf[:, :], in_=buf[:, :],
                                                              func=AF.Silu),
                    reads=[nm], writes=[nm])

    def emit_chunk_group(w, PP):
        g0, g1 = w * (QW // C), (w + 1) * (QW // C)
        for cc in range(g0, g1):
            A, B = Rec(), Rec()
            emit_chunk_head(cc, A)
            emit_chunk_state(cc, A)
            if cc >= 1:
                emit_chunk_tail(cc - 1, B)
            replay_interleaved(PP, [A, B])

    for w in range(NW):
        R1, R2 = Rec(), Rec()
        emit_p1_window(w, R1)
        if w >= 1:
            emit_chunk_group(w - 1, R2)
        replay_interleaved(P, [R1, R2])
    emit_chunk_group(NW - 1, P)
    emit_chunk_tail(dm.NCH - 1, P)

    KSEL = float(3 * dm.T // 4)
    P.emit("gpsimd", lambda e: e.memset(lo_t[:, :], -64.0), writes=["lo_t"])
    P.emit("gpsimd", lambda e: e.memset(hi_t[:, :], 64.0), writes=["hi_t"])
    P.emit("gpsimd", lambda e: e.memset(onescol[:, :], 1.0), writes=["onescol"])
    for _it in range(46):
        P.emit("vector", lambda e: e.tensor_scalar(out=mid_t[:, :], in0=lo_t[:, :], scalar1=hi_t[0:1, 0:1],
                                                   scalar2=0.5, op0=ALU.add, op1=ALU.mult),
               reads=["lo_t", "hi_t"], writes=["mid_t"])
        P.emit("tensor", lambda e: e.matmul(pb[6][:, 0:1], lhsT=ones128[:, :], rhs=mid_t[0:1, 0:1],
                                            start=True, stop=True),
               reads=["ones128", "mid_t"], writes=["b6"])
        P.emit("vector", lambda e: e.tensor_copy(mid128[:, :], pb[6][:, 0:1]), reads=["b6"],
               writes=["mid128"])
        P.emit("vector", lambda e: e.tensor_scalar(out=dead64[:, :], in0=scores[:, :],
                                                   scalar1=mid128[:, 0:1],
                                                   scalar2=None, op0=ALU.is_gt, op1=ALU.add,
                                                   accum_out=cnt_c[:, :]),
               reads=["scores", "mid128"], writes=["dead64", "cnt_c"])
        P.emit("tensor", lambda e: e.matmul(pb[6][0:1, 1:2], lhsT=cnt_c[:, 0:1], rhs=onescol[:, 0:1],
                                            start=True, stop=True),
               reads=["cnt_c", "onescol"], writes=["b6"])
        P.emit("vector", lambda e: e.tensor_copy(tot_t[:, :], pb[6][0:1, 1:2]), reads=["b6"],
               writes=["tot_t"])
        P.emit("vector", lambda e: e.tensor_scalar(out=sel_t[:, :], in0=tot_t[:, :], scalar1=KSEL - 0.5,
                                                   scalar2=None, op0=ALU.is_ge),
               reads=["tot_t"], writes=["sel_t"])
        P.emit("vector", lambda e: e.tensor_scalar(out=isel_t[:, :], in0=sel_t[:, :], scalar1=-1.0,
                                                   scalar2=1.0, op0=ALU.mult, op1=ALU.add),
               reads=["sel_t"], writes=["isel_t"])
        P.emit("vector", lambda e: e.tensor_sub(d1_t[:, :], mid_t[:, :], lo_t[:, :]),
               reads=["mid_t", "lo_t"], writes=["d1_t"])
        P.emit("vector", lambda e: e.tensor_sub(d2_t[:, :], mid_t[:, :], hi_t[:, :]),
               reads=["mid_t", "hi_t"], writes=["d2_t"])
        P.emit("vector", lambda e: e.scalar_tensor_tensor(out=lo_t[:, :], in0=d1_t[:, :],
                                                          scalar=sel_t[0:1, 0:1], in1=lo_t[:, :],
                                                          op0=ALU.mult, op1=ALU.add),
               reads=["d1_t", "sel_t", "lo_t"], writes=["lo_t"])
        P.emit("vector", lambda e: e.scalar_tensor_tensor(out=hi_t[:, :], in0=d2_t[:, :],
                                                          scalar=isel_t[0:1, 0:1], in1=hi_t[:, :],
                                                          op0=ALU.mult, op1=ALU.add),
               reads=["d2_t", "isel_t", "hi_t"], writes=["hi_t"])
    P.emit("tensor", lambda e: e.matmul(pb[6][:, 2:3], lhsT=ones128[:, :], rhs=lo_t[0:1, 0:1],
                                        start=True, stop=True),
           reads=["ones128", "lo_t"], writes=["b6"])
    P.emit("vector", lambda e: e.tensor_copy(thr128[:, :], pb[6][:, 2:3]), reads=["b6"],
           writes=["thr128"])

    rg = [[0, 1, 2, 3], [4, 5, 6, 7]]
    P.emit("gpsimd", lambda e: e.collective_compute("AllGather", ALU.bypass, ins=[agin[:, :]],
                                                    outs=[agout[:, :]], replica_groups=rg),
           reads=["agin0", "agin1"], writes=["agout"])

    all_bufs = list(set(list(P.last_write.keys()) + list(P.readers.keys())))
    P.emit("gpsimd", lambda e: e.memset(fencet[:, :], 0.0), writes=all_bufs + ["fence"])
    stack1.close()

    wbx = [sb(f"wb{dc}", [128, D]) for dc in range(dm.DC)]
    wox = [sb(f"wo{dc}", [128, D]) for dc in range(dm.DC)]
    xt2 = [sb(f"p5xt{i}", [128, D]) for i in range(2)]
    ot2 = [sb(f"p5ot{i}", [128, D]) for i in range(2)]
    xn2 = [sb(f"p5xn{i}", [128, D]) for i in range(2)]
    lnp2 = make_ln_scratch(sb, "b", 2)
    xnT2 = [sb(f"p5xnT{i}", [128, D]) for i in range(2)]
    oT2 = [sb(f"p5oT{i}", [128, D]) for i in range(2)]

    def mk_l2s(p):
        d = {}
        for nm in ("gel", "asil", "gpass", "dtmp", "g_t", "t3", "yt"):
            d[nm] = sb(f"l2{p}_{nm}", [128, D])
        d["dmask"] = sb(f"l2{p}_dmask", [128, 1])
        d["sc_c"] = sb(f"l2{p}_sc_c", [128, 1])
        return d
    l2s = [mk_l2s(0), mk_l2s(1)]

    ixt = sb("ixt", [128, 4 * NC5], mybir.dt.int32)
    wr128b = sb("wr128b", [128, D])
    dead5 = sb("dead5", [128, D])
    P.dma("sync", lambda e: e.dma_start(out=ixt[:, :], in_=agidx[:, :]),
          reads=["fence"], writes=["ixt"])
    P.dma("sync", lambda e: e.dma_start(out=wr128b[:, :], in_=wr.to_broadcast([128, D])),
          reads=["fence"], writes=["wr128b"])
    for dc in range(dm.DC):
        P.dma("sync", lambda e, dc=dc: e.dma_start(out=wbx[dc][:, :], in_=wbd[dc * 128:(dc + 1) * 128, :]),
              reads=["fence"], writes=[f"wb{dc}"])
        P.dma("sync", lambda e, dc=dc: e.dma_start(out=wox[dc][:, :], in_=wod[dc * 128:(dc + 1) * 128, :]),
              reads=["fence"], writes=[f"wo{dc}"])


    psT5 = (pb[4], pb[5])

    def transpose_into5(PP, src_name, srct, dst_name, dst):
        for g in range(dm.DC // 4):
            bank = "b4" if g % 2 == 0 else "b5"
            ps = psT5[0] if g % 2 == 0 else psT5[1]
            for jj in range(4):
                dc = g * 4 + jj
                PP.emit("tensor", lambda e, dc=dc, jj=jj, ps=ps, srct=srct: e.transpose(
                    ps[:, jj * 128:(jj + 1) * 128], srct[:, dc * 128:(dc + 1) * 128], t["ident"][:, :]),
                    reads=[src_name, "ident"], writes=[bank])
            eng = "vector" if g % 2 == 0 else "scalar"

            def _evac2(e, g=g, ps=ps, dst=dst):
                d = dst[:, g * 512:(g + 1) * 512]
                return e.tensor_copy(d, ps[:, :]) if hasattr(e, "tensor_copy") else e.copy(d, ps[:, :])
            PP.emit(eng, _evac2, reads=[bank], writes=[dst_name])

    def emit_p5_front(ci, PP):
        par = ci % 2
        s = l2s[par]
        n = {k: f"l2{par}_{k}" for k in s}
        xtn, otn, xnn = f"p5xt{par}", f"p5ot{par}", f"p5xn{par}"
        xnTn, oTn = f"p5xnT{par}", f"p5oT{par}"
        PP.dma("sync", lambda e, ci=ci, par=par: e.dma_start(out=xt2[par][:, :],
                                                             in_=xs[ci * 128:(ci + 1) * 128, :]),
               reads=["fence"], writes=[xtn])
        for j in range(4):
            PP.dma("gpsimd", lambda e, ci=ci, par=par, j=j: e.indirect_dma_start(
                out=ot2[par][:, j * V:(j + 1) * V], out_offset=None, in_=agout[:, :],
                in_offset=IndirectOffsetOnAxis(ixt[:, ci * 4 + j:ci * 4 + j + 1], 0)),
                reads=["agout", "ixt"], writes=[otn], dma_chan=f"dma_ot{par}")
        emit_ln_tile(PP, t, lnp2, "b", par, xtn, xt2[par][:, :], xnn, xn2[par][:, :], dm)
        PP.emit("vector", lambda e, par=par, s=s: e.scalar_tensor_tensor(
            out=dead5[:, :], in0=xn2[par][:, :], scalar=1.0, in1=wr128b[:, :],
            op0=ALU.bypass, op1=ALU.mult, accum_out=s["sc_c"][:, :]),
            reads=[xnn, "wr128b"], writes=["dead5", n["sc_c"]])
        PP.emit("scalar", lambda e, par=par, s=s: e.activation(out=s["gpass"][:, :], in_=xn2[par][:, :],
                                                               func=AF.Silu),
                reads=[xnn], writes=[n["gpass"]])
        transpose_into5(PP, xnn, xn2[par], xnTn, xnT2[par])
        transpose_into5(PP, otn, ot2[par], oTn, oT2[par])
        for eb in range(2):
            for dc in range(dm.DC):
                PP.emit("tensor", lambda e, dc=dc, eb=eb, par=par: e.matmul(
                    (pb[0] if eb == 0 else pb[1])[:, :], lhsT=xnT2[par][:, dc * 128:(dc + 1) * 128],
                    rhs=wbx[dc][:, eb * 512:(eb + 1) * 512],
                    start=(dc == 0), stop=(dc == dm.DC - 1)),
                    reads=[xnTn, f"wb{dc}"], writes=[f"b{eb}"])
        PP.emit("scalar", lambda e, s=s: e.activation(out=s["gel"][:, 0:512], in_=pb[0][:, :],
                                                      func=AF.Gelu_apprx_tanh),
                reads=["b0"], writes=[n["gel"]])
        PP.emit("scalar", lambda e, s=s: e.activation(out=s["gel"][:, 512:1024], in_=pb[1][:, :],
                                                      func=AF.Gelu_apprx_tanh),
                reads=["b1"], writes=[n["gel"]])

    def emit_p5_back(ci, PP):
        par = ci % 2
        s = l2s[par]
        n = {k: f"l2{par}_{k}" for k in s}
        xtn, oTn, xnTn = f"p5xt{par}", f"p5oT{par}", f"p5xnT{par}"
        PP.emit("scalar", lambda e, s=s: e.activation(out=s["asil"][:, :], in_=s["gel"][:, :],
                                                      func=AF.Silu),
                reads=[n["gel"]], writes=[n["asil"]])
        PP.emit("vector", lambda e, ci=ci, s=s: e.tensor_scalar(
            out=s["dmask"][:, :], in0=s["sc_c"][:, :],
            scalar1=thr128[:, 0:1], scalar2=None, op0=ALU.is_gt),
            reads=[n["sc_c"], "thr128"], writes=[n["dmask"]])
        PP.emit("vector", lambda e, s=s: e.tensor_sub(s["dtmp"][:, :], s["asil"][:, :], s["gpass"][:, :]),
                reads=[n["asil"], n["gpass"]], writes=[n["dtmp"]])
        PP.emit("vector", lambda e, s=s: e.scalar_tensor_tensor(out=s["g_t"][:, :], in0=s["dtmp"][:, :],
                                                                scalar=s["dmask"][:, 0:1],
                                                                in1=s["gpass"][:, :],
                                                                op0=ALU.mult, op1=ALU.add),
                reads=[n["dtmp"], n["dmask"], n["gpass"]], writes=[n["g_t"]])
        for eb in range(2):
            for dc in range(dm.DC):
                PP.emit("tensor", lambda e, dc=dc, eb=eb, par=par: e.matmul(
                    (pb[2] if eb == 0 else pb[3])[:, :], lhsT=oT2[par][:, dc * 128:(dc + 1) * 128],
                    rhs=wox[dc][:, eb * 512:(eb + 1) * 512],
                    start=(dc == 0), stop=(dc == dm.DC - 1)),
                    reads=[oTn, f"wo{dc}"], writes=[f"b{eb + 2}"])
        for eb in range(2):
            PP.emit("vector", lambda e, eb=eb, s=s: e.tensor_mul(
                s["t3"][:, eb * 512:(eb + 1) * 512], s["g_t"][:, eb * 512:(eb + 1) * 512],
                (pb[2] if eb == 0 else pb[3])[:, :]),
                reads=[n["g_t"], f"b{eb + 2}"], writes=[n["t3"]])
        PP.emit("vector", lambda e, par=par, s=s: e.tensor_add(s["yt"][:, :], s["t3"][:, :],
                                                               xt2[par][:, :]),
                reads=[n["t3"], xtn], writes=[n["yt"]])
        PP.dma("sync", lambda e, ci=ci, s=s: e.dma_start(out=y[ci * 128:(ci + 1) * 128, :],
                                                         in_=s["yt"][:, :]),
               reads=[n["yt"]], writes=["y"], dma_chan=f"dma_y{par}")

    for ci in range(NC5):
        A, B = Rec(), Rec()
        emit_p5_front(ci, A)
        if ci >= 1:
            emit_p5_back(ci - 1, B)
        replay_interleaved(P, [A, B])
    emit_p5_back(NC5 - 1, P)

    with nc.Block() as block:
        P.finalize_and_run(stack, block)
    return nc, stack


def kernel(L0_attn_x, L0_attn_Wq, L0_attn_Wk, L0_attn_Wv, L0_attn_Wf, L0_attn_g_norm_w,
           L0_attn_Wo, L0_ffn_w_router, L0_ffn_w_block):
    dm = DIMS
    x = np.ascontiguousarray(np.asarray(L0_attn_x, dtype=np.float32))
    Wq = np.asarray(L0_attn_Wq, dtype=np.float32)
    Wk = np.asarray(L0_attn_Wk, dtype=np.float32)
    Wv = np.asarray(L0_attn_Wv, dtype=np.float32)
    Wf = np.asarray(L0_attn_Wf, dtype=np.float32)
    gnw = np.asarray(L0_attn_g_norm_w, dtype=np.float32)
    Wo = np.ascontiguousarray(np.asarray(L0_attn_Wo, dtype=np.float32))
    wrr = np.asarray(L0_ffn_w_router, dtype=np.float32)
    wb = np.ascontiguousarray(np.asarray(L0_ffn_w_block, dtype=np.float32))
    B = x.shape[0]

    nc, stack = build_fused(dm)
    in_maps = []
    for c in range(8):
        b, h = c // 4, c % 4
        wcat = np.concatenate([
            Wq[:, h * dm.KD:(h + 1) * dm.KD],
            Wk[:, h * dm.KD:(h + 1) * dm.KD],
            Wv[:, h * dm.V:(h + 1) * dm.V],
            Wf[:, h * dm.M:(h + 1) * dm.M],
        ], axis=1)
        in_maps.append({
            "xb": np.ascontiguousarray(x[b]),
            "wcat": np.ascontiguousarray(wcat),
            "wr": np.ascontiguousarray(wrr[None, :]),
            "gnwd": np.ascontiguousarray(gnw[None, :]),
            "wbd": wb,
            "wod": Wo,
            "xs": np.ascontiguousarray(x[b, h * dm.SL:(h + 1) * dm.SL]),
            "agidx": np.ascontiguousarray(
                (np.arange(128, dtype=np.int32)[:, None]
                 + (np.arange(dm.SL // 128 * 4, dtype=np.int32)[None, :] // 4) * 128
                 + (np.arange(dm.SL // 128 * 4, dtype=np.int32)[None, :] % 4) * dm.T
                 + h * dm.SL).astype(np.int32)),
        })
    res = run_bass_kernel_spmd(nc, in_maps, core_ids=list(range(8)))
    yv = np.empty((B, dm.T, dm.D), dtype=np.float32)
    for c in range(8):
        b, h = c // 4, c % 4
        yv[b, h * dm.SL:(h + 1) * dm.SL] = np.asarray(res.results[c]["y"])
    stack.close()
    return yv


# version 14
# speedup vs baseline: 1.1624x; 1.1624x over previous
import numpy as np
import concourse.bass as bass
from concourse.bass_core_rust import IndirectOffsetOnAxis
import concourse.mybir as mybir
from concourse.bass_utils import run_bass_kernel_spmd

F32 = mybir.dt.float32
F16 = mybir.dt.float16
AF = mybir.ActivationFunctionType
ALU = mybir.AluOpType
AX = mybir.AxisListType

EPS = 1e-5


class Dims:
    def __init__(self, T=8192, D=1024, KD=256, M=64, V=256, TBLK=512):
        self.T = T
        self.D = D
        self.KD = KD
        self.M = M
        self.V = V
        self.C = 128
        self.TBLK = TBLK
        self.NTB = T // TBLK
        self.NCH = T // self.C
        self.DC = D // 128
        self.JCOLS = 3 * KD + M
        self.NSL = 4
        self.SL = T // 4


DIMS = Dims()


ENGINES = ("sync", "scalar", "tensor", "vector", "gpsimd")


class _Op:
    __slots__ = ("engine", "build", "deps", "is_dma", "chan", "needs_inc", "seq", "tag", "no_inc")

    def __init__(self, engine, build, is_dma, chan, tag, no_inc=False):
        self.engine = engine
        self.build = build
        self.deps = []
        self.is_dma = is_dma
        self.chan = chan
        self.needs_inc = False
        self.seq = None
        self.tag = tag
        self.no_inc = no_inc


class Prog:
    def __init__(self, nc):
        self.nc = nc
        self.ops = []
        self.last_write = {}
        self.readers = {}

    def emit(self, engine, build, reads=(), writes=(), is_dma=False, tag="", dma_chan=None,
             no_inc=False):
        if is_dma:
            chan = dma_chan or ("dma_" + (writes[0] if writes else reads[0]))
        else:
            chan = engine
        op = _Op(engine, build, is_dma, chan, tag, no_inc)
        deps = []

        def need(other):
            return not (other.is_dma and is_dma and other.chan == chan)

        for b in reads:
            w = self.last_write.get(b)
            if w is not None and need(w):
                deps.append(w)
        for b in writes:
            w = self.last_write.get(b)
            if w is not None and need(w):
                deps.append(w)
            for r in self.readers.get(b, ()):
                if need(r):
                    deps.append(r)
        op.deps = deps
        for b in reads:
            self.readers.setdefault(b, []).append(op)
        for b in writes:
            self.last_write[b] = op
            self.readers[b] = []
        self.ops.append(op)
        return op

    def dma(self, engine, build, reads=(), writes=(), tag="", dma_chan=None):
        return self.emit(engine, build, reads, writes, is_dma=True, tag=tag, dma_chan=dma_chan)

    def finalize_and_run(self, stack, block):
        nc = self.nc
        for op in self.ops:
            for d in op.deps:
                if not d.no_inc:
                    d.needs_inc = True
        for op in self.ops:
            if op.is_dma:
                op.needs_inc = True
        counts = {}
        for op in self.ops:
            if op.needs_inc:
                amt = 16 if op.is_dma else 1
                counts[op.chan] = counts.get(op.chan, 0) + amt
                op.seq = counts[op.chan]
        chans = {c: stack.enter_context(nc.semaphore(("ch_" + c)[:48])) for c in counts}
        dma_finals = {c: v for c, v in counts.items() if c.startswith("dma_")}

        def mk(engine_name):
            def body(eng):
                for op in self.ops:
                    if op.engine != engine_name:
                        continue
                    w = {}
                    for d in op.deps:
                        if d.seq is None:
                            continue
                        w[d.chan] = max(w.get(d.chan, 0), d.seq)
                    for c, v in w.items():
                        eng.wait_ge(chans[c], v)
                    inst = op.build(eng)
                    if op.needs_inc and not op.no_inc:
                        inst.then_inc(chans[op.chan], 16 if op.is_dma else 1)
                if engine_name == "sync":
                    for c, v in dma_finals.items():
                        eng.wait_ge(chans[c], v)
            return body

        block.sync(mk("sync"))
        block.scalar(mk("scalar"))
        block.tensor(mk("tensor"))
        block.vector(mk("vector"))
        block.gpsimd(mk("gpsimd"))


class Rec:

    def __init__(self):
        self.items = []

    def emit(self, *a, **kw):
        self.items.append(("emit", a, kw))

    def dma(self, *a, **kw):
        self.items.append(("dma", a, kw))


def replay_interleaved(P, recs):
    recs = [r for r in recs if r.items]
    k = len(recs)
    idx = [0] * k
    while True:
        best, bf = -1, 2.0
        for i in range(k):
            if idx[i] < len(recs[i].items):
                f = idx[i] / len(recs[i].items)
                if f < bf:
                    bf, best = f, i
        if best < 0:
            break
        kind, a, kw = recs[best].items[idx[best]]
        idx[best] += 1
        (P.emit if kind == "emit" else P.dma)(*a, **kw)


def emit_consts(P, t):
    P.emit("gpsimd", lambda e: e.memset(t["ident"][:, :], 0.0), writes=["ident"])
    P.emit("gpsimd", lambda e: e.affine_select(out=t["ident"][:, :], in_=t["ident"][:, :],
                                               compare_op=ALU.not_equal, fill=1.0, base=0,
                                               pattern=[[-1, 128]], channel_multiplier=1),
           reads=["ident"], writes=["ident"])
    P.emit("gpsimd", lambda e: e.memset(t["cb_one"][:, :], 1.0), writes=["cb_one"])
    P.emit("gpsimd", lambda e: e.memset(t["cb_eps"][:, :], EPS), writes=["cb_eps"])


def make_ln_scratch(sb, tag, n=2):
    out = []
    for p in range(n):
        d = {}
        d["bno"] = sb(f"bno{tag}{p}", [128, 12])
        d["bna"] = sb(f"bna{tag}{p}", [128, 2])
        d["lnv_c"] = sb(f"lnvc{tag}{p}", [128, 1])
        d["r_c"] = sb(f"rc{tag}{p}", [128, 1])
        d["nmr_c"] = sb(f"nmrc{tag}{p}", [128, 1])
        out.append(d)
    return out


def emit_ln_tile(P, t, lnp, tag, par, xt_name, xt_ap, xn_name, xn_ap, dm):
    s = lnp[par]
    n = {k: f"ln{tag}{par}_{k}" for k in s}
    bno, bna = s["bno"], s["bna"]
    P.emit("vector", lambda e: e.bn_stats(bno[:, 0:6], xt_ap[:, 0:512]),
           reads=[xt_name], writes=[n["bno"]])
    P.emit("vector", lambda e: e.bn_stats(bno[:, 6:12], xt_ap[:, 512:1024]),
           reads=[xt_name], writes=[n["bno"]])
    P.emit("vector", lambda e: e.bn_aggr(bna[:, :], bno[:, :]), reads=[n["bno"]], writes=[n["bna"]])
    P.emit("scalar", lambda e: e.activation(out=s["lnv_c"][:, :], in_=bna[:, 1:2], func=AF.Ln,
                                            bias=t["cb_eps"][:, 0:1]),
           reads=[n["bna"], "cb_eps"], writes=[n["lnv_c"]])
    P.emit("scalar", lambda e: e.activation(out=s["r_c"][:, :], in_=s["lnv_c"][:, :], func=AF.Exp,
                                            scale=-0.5),
           reads=[n["lnv_c"]], writes=[n["r_c"]])
    P.emit("vector", lambda e: e.scalar_tensor_tensor(out=s["nmr_c"][:, :], in0=bna[:, 0:1],
                                                      scalar=-1.0, in1=s["r_c"][:, :],
                                                      op0=ALU.mult, op1=ALU.mult),
           reads=[n["bna"], n["r_c"]], writes=[n["nmr_c"]])
    P.emit("scalar", lambda e: e.activation(out=xn_ap, in_=xt_ap, func=AF.Identity,
                                            scale=s["r_c"][:, 0:1], bias=s["nmr_c"][:, 0:1]),
           reads=[xt_name, n["r_c"], n["nmr_c"]], writes=[xn_name])


def build_fused(dm):
    from contextlib import ExitStack
    nc = bass.Bass("TRN2", target_bir_lowering=False)
    stack = ExitStack()
    T, D, KD, M, V, C = dm.T, dm.D, dm.KD, dm.M, dm.V, dm.C
    SL = dm.SL
    NC5 = SL // 128

    xb = nc.dram_tensor("xb", [T, D], F32, kind="ExternalInput").ap()
    wcat = nc.dram_tensor("wcat", [D, dm.JCOLS], F32, kind="ExternalInput").ap()
    wr = nc.dram_tensor("wr", [1, D], F32, kind="ExternalInput").ap()
    gnwd = nc.dram_tensor("gnwd", [1, V], F32, kind="ExternalInput").ap()
    wbd = nc.dram_tensor("wbd", [D, D], F32, kind="ExternalInput").ap()
    wod = nc.dram_tensor("wod", [D, D], F32, kind="ExternalInput").ap()
    xs = nc.dram_tensor("xs", [SL, D], F32, kind="ExternalInput").ap()
    agidx = nc.dram_tensor("agidx", [128, 4 * NC5], mybir.dt.int32, kind="ExternalInput").ap()
    y = nc.dram_tensor("y", [SL, D], F32, kind="ExternalOutput").ap()
    agin = nc.dram_tensor("agin", [T, V], F32, kind="Internal").ap()
    xndh = nc.dram_tensor("xndh", [T, D], F16, kind="Internal").ap()
    xndl = nc.dram_tensor("xndl", [T, D], F16, kind="Internal").ap()
    agout = nc.dram_tensor("agout", [4 * T, V], F32, kind="Internal").ap()

    sb = lambda name, shape, dt=F32: stack.enter_context(nc.sbuf_tensor(name, shape, dt))
    thr128 = sb("thr128", [128, 1])
    lo_t = sb("lo_t", [1, 1]); hi_t = sb("hi_t", [1, 1]); mid_t = sb("mid_t", [1, 1])
    sel_t = sb("sel_t", [1, 1]); isel_t = sb("isel_t", [1, 1])
    d1_t = sb("d1_t", [1, 1]); d2_t = sb("d2_t", [1, 1]); tot_t = sb("tot_t", [1, 1])
    mid128 = sb("mid128", [128, 1]); cnt_c = sb("cnt_c", [128, 1])
    onescol = sb("onescol", [128, 1]); dead64 = sb("dead64", [128, T // 128])
    fencet = sb("fencet", [1, 1])
    t = {}
    t["ident"] = sb("ident", [128, 128])
    t["cb_one"] = sb("cb_one", [128, 1])
    t["cb_eps"] = sb("cb_eps", [128, 1])
    mask01 = sb("mask01", [128, 128])
    ones128 = sb("ones128", [1, 128])
    zcol = sb("zcol", [M, 1])
    gnw = sb("gnw", [128, V])
    scores = sb("scores", [128, T // 128])
    stack1 = ExitStack()
    sb1 = lambda name, shape, dt=F32: stack1.enter_context(nc.sbuf_tensor(name, shape, dt))
    wr128 = sb1("wr128", [128, D])
    NXT = 2
    xt = [sb1(f"xt{i}", [128, D]) for i in range(NXT)]
    xn = [sb1(f"xn{i}", [128, D]) for i in range(NXT)]
    lnp = make_ln_scratch(sb1, "a", 2)
    dead_v = [sb1("dead_v0", [128, D])] * 2
    WIN = 2 * dm.TBLK
    xnTh = sb1("xnTh", [128, dm.DC * WIN], F16)
    xnTl = sb1("xnTl", [128, dm.DC * WIN], F16)
    xnhi = [sb1(f"xnhi{i}", [128, D], F16) for i in range(2)]
    xnlo = [sb1(f"xnlo{i}", [128, D], F16) for i in range(2)]
    wch = [sb1(f"wch{dc}", [128, dm.JCOLS], F16) for dc in range(dm.DC)]
    wcl = [sb1(f"wcl{dc}", [128, dm.JCOLS], F16) for dc in range(dm.DC)]
    QW = min(1024, T)
    qf = [sb1(f"qf{i}", [128, 2 * QW]) for i in range(2)]
    kf = [sb1(f"kf{i}", [128, 2 * QW]) for i in range(2)]
    vf = [sb1(f"vf{i}", [128, 2 * QW]) for i in range(2)]
    useg = sb1("useg", [M, dm.TBLK])
    Lseg = sb1("Lseg", [M, dm.TBLK])
    Bg = sb1("Bg", [M, T])
    hkr = [[sb1(f"hk{r}_0", [128, M]), sb1(f"hk{r}_1", [128, M])] for r in range(3)]
    hvr = [sb1(f"hv{r}", [M, V]) for r in range(3)]

    def mk_cs(p):
        d = {}
        for nm, shape in (("bias_c", [M, 1]), ("en", [M, C]), ("stT", [M, C]), ("brel", [M, C]),
                          ("eend", [M, 1]), ("eer", [1, M]), ("eer128", [128, M]), ("E", [128, M]),
                          ("stil", [128, M]), ("atm", [128, 128]), ("logits", [128, M]),
                          ("rmax", [128, 1]), ("nb16", [128, 1]), ("p_sb", [128, M]),
                          ("den", [128, 1]), ("rden", [128, 1]), ("pe", [128, M]),
                          ("pet", [M, 128]), ("gtm", [128, 128]), ("vt", [128, V]),
                          ("kt", [128, KD]), ("o32", [128, V]), ("osq", [128, V]),
                          ("ssq", [128, 1]), ("lnv", [128, 1]), ("rstd", [128, 1]),
                          ("on", [128, V]), ("t1a", [128, M]), ("t1b", [128, M]), ("t2", [M, V])):
            d[nm] = sb1(f"cs{p}_{nm}", shape)
        return d
    cs = [mk_cs(0), mk_cs(1)]

    pb = [stack.enter_context(nc.psum_tensor(f"pb{i}", [128, 512], F32)) for i in range(8)]
    ps_proj = pb[0]
    psT = (pb[0], pb[0])

    def mk_ps(base):
        bsm, bvk, bop = base
        return {
            "at": pb[bsm][:, 0:128], "gt": pb[bsm][:, 128:256],
            "logit": pb[bsm][:, 256:320], "brelT": pb[bsm][:, 320:384],
            "eer": pb[bsm][0:1, 384:448], "eerep": pb[bsm][:, 448:512],
            "stil": pb[bop][:, 384:448],
            "vt": pb[bvk][:, 0:256], "kt": pb[bvk][:, 256:512],
            "o": pb[bop][:, 0:256], "pet": pb[bop][0:M, 256:384],
            "bsm": f"b{bsm}", "bvk": f"b{bvk}", "bop": f"b{bop}",
        }
    psc = [mk_ps((1, 2, 3)), mk_ps((4, 5, 6))]
    ps_hk = pb[7][:, 0:128]
    ps_hv = pb[7][0:M, 128:384]

    P = Prog(nc)

    emit_consts(P, t)
    P.emit("gpsimd", lambda e: e.memset(mask01[:, :], 1.0), writes=["mask01"])
    P.emit("gpsimd", lambda e: e.affine_select(out=mask01[:, :], in_=mask01[:, :], compare_op=ALU.is_ge,
                                               fill=0.0, base=0, pattern=[[1, 128]], channel_multiplier=-1),
           reads=["mask01"], writes=["mask01"])
    P.emit("gpsimd", lambda e: e.memset(ones128[:, :], 1.0), writes=["ones128"])
    P.emit("gpsimd", lambda e: e.memset(zcol[:, :], 0.0), writes=["zcol"])
    P.emit("gpsimd", lambda e: e.memset(hkr[2][0][:, :], 0.0), writes=["hk2_0"])
    P.emit("gpsimd", lambda e: e.memset(hkr[2][1][:, :], 0.0), writes=["hk2_1"])
    P.emit("gpsimd", lambda e: e.memset(hvr[2][:, :], 0.0), writes=["hv2"])
    P.dma("sync", lambda e: e.dma_start(out=gnw[:, :], in_=gnwd.to_broadcast([128, V])), writes=["gnw"])
    P.dma("sync", lambda e: e.dma_start(out=wr128[:, :], in_=wr.to_broadcast([128, D])), writes=["wr128"])
    wcs = dead_v[0][:, 0:dm.JCOLS]
    for dc in range(dm.DC):
        P.dma("sync", lambda e, dc=dc: e.dma_start(out=wcs, in_=wcat[dc * 128:(dc + 1) * 128, :]),
              writes=["dead_v0"])
        P.emit("vector", lambda e, dc=dc: e.tensor_copy(wch[dc][:, :], wcs),
               reads=["dead_v0"], writes=[f"wch{dc}"])
        P.emit("vector", lambda e, dc=dc: e.tensor_sub(wcl[dc][:, :], wcs, wch[dc][:, :]),
               reads=["dead_v0", f"wch{dc}"], writes=[f"wcl{dc}"])

    def chunk_env(c):
        par = c % 2
        s = cs[par]
        ps = psc[par]
        n = {k: f"c{par}_{k}" for k in s}
        qwin = (c * C // QW) % 2
        return par, s, ps, n, qwin

    def emit_chunk_head(c, PP):
        c0 = c * C
        par, s, ps, n, qwin = chunk_env(c)
        bsm, bvk, bop = ps["bsm"], ps["bvk"], ps["bop"]
        off = (c * C) % QW
        Bc = Bg[:, c0 - 1:c0] if c > 0 else zcol[:, :]
        bsrc = ["Bg"] if c > 0 else ["zcol"]
        qfp, kfp, vfp = qf[qwin], kf[qwin], vf[qwin]
        qn, kn, vn = f"qf{qwin}", f"kf{qwin}", f"vf{qwin}"
        hki = hkr[(c - 1) % 3]
        hkin = [f"hk{(c - 1) % 3}_0", f"hk{(c - 1) % 3}_1"]
        PP.emit("vector", lambda e: e.tensor_scalar_mul(s["bias_c"][:, :], Bc, -1.0 / 8.0),
               reads=bsrc, writes=[n["bias_c"]])
        PP.emit("scalar", lambda e: e.activation(out=s["en"][:, :], in_=Bg[:, c0:c0 + C], func=AF.Exp,
                                                scale=1.0 / 8.0, bias=s["bias_c"][:, 0:1]),
               reads=["Bg", n["bias_c"]], writes=[n["en"]])
        PP.emit("vector", lambda e: e.tensor_scalar_add(s["stT"][:, 0:1], s["en"][:, 0:1], -1.0),
               reads=[n["en"]], writes=[n["stT"]])
        PP.emit("vector", lambda e: e.tensor_sub(s["stT"][:, 1:C], s["en"][:, 1:C], s["en"][:, 0:C - 1]),
               reads=[n["en"]], writes=[n["stT"]])
        PP.emit("vector", lambda e: e.tensor_scalar(out=s["brel"][:, :], in0=Bg[:, c0:c0 + C],
                                                   scalar1=Bc, scalar2=None, op0=ALU.subtract),
               reads=["Bg"] + bsrc, writes=[n["brel"]])
        PP.emit("scalar", lambda e: e.activation(out=s["eend"][:, :], in_=s["brel"][:, C - 1:C],
                                                func=AF.Exp, scale=-1.0 / 8.0),
               reads=[n["brel"]], writes=[n["eend"]])
        PP.emit("tensor", lambda e: e.transpose(ps["brelT"], s["brel"][:, :], t["ident"][0:M, 0:M]),
               reads=[n["brel"], "ident"], writes=[bsm])
        PP.emit("scalar", lambda e: e.activation(out=s["E"][:, :], in_=ps["brelT"], func=AF.Exp,
                                                scale=-1.0 / 8.0),
               reads=[bsm], writes=[n["E"]])
        PP.emit("tensor", lambda e: e.transpose(ps["eer"], s["eend"][:, :], t["ident"][0:M, 0:M]),
               reads=[n["eend"], "ident"], writes=[bsm])
        PP.emit("vector", lambda e: e.tensor_copy(s["eer"][:, :], ps["eer"]), reads=[bsm], writes=[n["eer"]])
        PP.emit("tensor", lambda e: e.matmul(ps["eerep"], lhsT=ones128[:, :], rhs=s["eer"][:, :],
                                            start=True, stop=True),
               reads=["ones128", n["eer"]], writes=[bsm])
        PP.emit("vector", lambda e: e.tensor_copy(s["eer128"][:, :], ps["eerep"]), reads=[bsm],
               writes=[n["eer128"]])
        PP.emit("tensor", lambda e: e.transpose(ps["stil"], s["stT"][:, :], t["ident"][0:M, 0:M]),
               reads=[n["stT"], "ident"], writes=[bop])
        PP.emit("vector", lambda e: e.tensor_copy(s["stil"][:, :], ps["stil"]), reads=[bop],
               writes=[n["stil"]])
        PP.emit("tensor", lambda e: e.matmul(ps["at"], lhsT=kfp[:, off:off + C], rhs=qfp[:, off:off + C],
                                            start=True, stop=False),
               reads=[kn, qn], writes=[bsm])
        PP.emit("tensor", lambda e: e.matmul(ps["at"], lhsT=kfp[:, QW + off:QW + off + C],
                                            rhs=qfp[:, QW + off:QW + off + C], start=False, stop=True),
               reads=[kn, qn], writes=[bsm])
        PP.emit("vector", lambda e: e.scalar_tensor_tensor(out=s["atm"][:, :], in0=ps["at"], scalar=1.0,
                                                          in1=mask01[:, :], op0=ALU.bypass, op1=ALU.mult),
               reads=[bsm, "mask01"], writes=[n["atm"]])
        PP.emit("tensor", lambda e: e.matmul(ps["logit"], lhsT=s["atm"][:, :], rhs=s["stil"][:, :],
                                            start=True, stop=False),
               reads=[n["atm"], n["stil"]], writes=[bsm])
        PP.emit("tensor", lambda e: e.matmul(ps["logit"], lhsT=qfp[:, off:off + C], rhs=hki[0][:, :],
                                            start=False, stop=False),
               reads=[qn, hkin[0]], writes=[bsm])
        PP.emit("tensor", lambda e: e.matmul(ps["logit"], lhsT=qfp[:, QW + off:QW + off + C],
                                            rhs=hki[1][:, :], start=False, stop=True),
               reads=[qn, hkin[1]], writes=[bsm])
        PP.emit("tensor", lambda e: e.transpose(ps["vt"][:, 0:128], vfp[:, off:off + C], t["ident"][:, :]),
               reads=[vn, "ident"], writes=[bvk])
        PP.emit("tensor", lambda e: e.transpose(ps["vt"][:, 128:256], vfp[:, QW + off:QW + off + C],
                                               t["ident"][:, :]),
               reads=[vn, "ident"], writes=[bvk])
        PP.emit("scalar", lambda e: e.copy(s["vt"][:, :], ps["vt"]), reads=[bvk], writes=[n["vt"]])
        PP.emit("tensor", lambda e: e.transpose(ps["kt"][:, 0:128], kfp[:, off:off + C], t["ident"][:, :]),
               reads=[kn, "ident"], writes=[bvk])
        PP.emit("tensor", lambda e: e.transpose(ps["kt"][:, 128:256], kfp[:, QW + off:QW + off + C],
                                               t["ident"][:, :]),
               reads=[kn, "ident"], writes=[bvk])
        PP.emit("scalar", lambda e: e.copy(s["kt"][:, :], ps["kt"]), reads=[bvk], writes=[n["kt"]])

    def emit_chunk_state(c, PP):
        par, s, ps, n, qwin = chunk_env(c)
        hki = hkr[(c - 1) % 3]
        hkin = [f"hk{(c - 1) % 3}_0", f"hk{(c - 1) % 3}_1"]
        hko = hkr[c % 3]
        hkon = [f"hk{c % 3}_0", f"hk{c % 3}_1"]
        hvi, hvin = hvr[(c - 1) % 3], f"hv{(c - 1) % 3}"
        hvo, hvon = hvr[c % 3], f"hv{c % 3}"
        PP.emit("tensor", lambda e: e.matmul(ps_hk[:, 0:64], lhsT=s["kt"][:, 0:128], rhs=s["stil"][:, :],
                                            start=True, stop=True),
               reads=[n["kt"], n["stil"]], writes=["b7"])
        PP.emit("tensor", lambda e: e.matmul(ps_hk[:, 64:128], lhsT=s["kt"][:, 128:256],
                                            rhs=s["stil"][:, :], start=True, stop=True),
               reads=[n["kt"], n["stil"]], writes=["b7"])
        for i, tn in ((0, "t1a"), (1, "t1b")):
            PP.emit("vector", lambda e, i=i, tn=tn: e.tensor_add(s[tn][:, :],
                                                                ps_hk[:, 64 * i:64 * (i + 1)],
                                                                hki[i][:, :]),
                   reads=["b7", hkin[i]], writes=[n[tn]])
            PP.emit("vector", lambda e, i=i, tn=tn: e.tensor_mul(hko[i][:, :], s[tn][:, :],
                                                                s["eer128"][:, :]),
                   reads=[n[tn], n["eer128"]], writes=[hkon[i]])
        PP.emit("tensor", lambda e: e.matmul(ps_hv, lhsT=s["stil"][:, :], rhs=s["vt"][:, :],
                                            start=True, stop=True),
               reads=[n["stil"], n["vt"]], writes=["b7"])
        PP.emit("vector", lambda e: e.tensor_add(s["t2"][:, :], ps_hv, hvi[:, :]),
               reads=["b7", hvin], writes=[n["t2"]])
        PP.emit("vector", lambda e: e.tensor_scalar(out=hvo[:, :], in0=s["t2"][:, :],
                                                   scalar1=s["eend"][:, 0:1], scalar2=None, op0=ALU.mult),
               reads=[n["t2"], n["eend"]], writes=[hvon])

    def emit_chunk_tail(c, PP):
        c0 = c * C
        par, s, ps, n, qwin = chunk_env(c)
        bsm, bvk, bop = ps["bsm"], ps["bvk"], ps["bop"]
        hvi, hvin = hvr[(c - 1) % 3], f"hv{(c - 1) % 3}"
        PP.emit("vector", lambda e: e.tensor_tensor(out=s["logits"][:, :], in0=ps["logit"],
                                                   in1=s["E"][:, :], op=ALU.mult),
               reads=[bsm, n["E"]], writes=[n["logits"]])
        PP.emit("vector", lambda e: e.reduce_max(s["rmax"][:, :], s["logits"][:, :], axis=AX.X),
               reads=[n["logits"]], writes=[n["rmax"]])
        PP.emit("vector", lambda e: e.tensor_scalar_mul(s["nb16"][:, :], s["rmax"][:, :], -1.0 / 16.0),
               reads=[n["rmax"]], writes=[n["nb16"]])
        PP.emit("scalar", lambda e: e.activation(out=s["p_sb"][:, :], in_=s["logits"][:, :], func=AF.Exp,
                                                scale=1.0 / 16.0, bias=s["nb16"][:, 0:1],
                                                accum_out=s["den"][:, :]),
               reads=[n["logits"], n["nb16"]], writes=[n["p_sb"], n["den"]])
        PP.emit("vector", lambda e: e.reciprocal(s["rden"][:, :], s["den"][:, :]),
               reads=[n["den"]], writes=[n["rden"]])
        PP.emit("vector", lambda e: e.scalar_tensor_tensor(out=s["pe"][:, :], in0=s["p_sb"][:, :],
                                                          scalar=s["rden"][:, 0:1], in1=s["E"][:, :],
                                                          op0=ALU.mult, op1=ALU.mult),
               reads=[n["p_sb"], n["rden"], n["E"]], writes=[n["pe"]])
        PP.emit("tensor", lambda e: e.transpose(ps["pet"], s["pe"][:, :], t["ident"][:, :]),
               reads=[n["pe"], "ident"], writes=[bop])
        PP.emit("vector", lambda e: e.tensor_copy(s["pet"][:, :], ps["pet"]), reads=[bop], writes=[n["pet"]])
        PP.emit("tensor", lambda e: e.matmul(ps["gt"], lhsT=s["stT"][:, :], rhs=s["pet"][:, :],
                                            start=True, stop=True),
               reads=[n["stT"], n["pet"]], writes=[bsm])
        PP.emit("vector", lambda e: e.scalar_tensor_tensor(out=s["gtm"][:, :], in0=ps["gt"], scalar=1.0,
                                                          in1=mask01[:, :], op0=ALU.bypass, op1=ALU.mult),
               reads=[bsm, "mask01"], writes=[n["gtm"]])
        PP.emit("tensor", lambda e: e.matmul(ps["o"], lhsT=s["gtm"][:, :], rhs=s["vt"][:, :],
                                            start=True, stop=False),
               reads=[n["gtm"], n["vt"]], writes=[bop])
        PP.emit("tensor", lambda e: e.matmul(ps["o"], lhsT=s["pet"][:, :], rhs=hvi[:, :],
                                            start=False, stop=True),
               reads=[n["pet"], hvin], writes=[bop])
        PP.emit("vector", lambda e: e.tensor_copy(s["o32"][:, :], ps["o"]), reads=[bop], writes=[n["o32"]])
        PP.emit("scalar", lambda e: e.activation(out=s["osq"][:, :], in_=s["o32"][:, :], func=AF.Square,
                                                accum_out=s["ssq"][:, :]),
               reads=[n["o32"]], writes=[n["osq"], n["ssq"]])
        PP.emit("scalar", lambda e: e.activation(out=s["lnv"][:, :], in_=s["ssq"][:, :], func=AF.Ln,
                                                scale=1.0 / V, bias=t["cb_eps"][:, 0:1]),
               reads=[n["ssq"], "cb_eps"], writes=[n["lnv"]])
        PP.emit("scalar", lambda e: e.activation(out=s["rstd"][:, :], in_=s["lnv"][:, :], func=AF.Exp,
                                                scale=-0.5),
               reads=[n["lnv"]], writes=[n["rstd"]])
        PP.emit("vector", lambda e: e.scalar_tensor_tensor(out=s["on"][:, :], in0=s["o32"][:, :],
                                                          scalar=s["rstd"][:, 0:1], in1=gnw[:, :],
                                                          op0=ALU.mult, op1=ALU.mult),
               reads=[n["o32"], n["rstd"], "gnw"], writes=[n["on"]])
        PP.dma("sync", lambda e: e.dma_start(out=agin[c0:c0 + C, :], in_=s["on"][:, :]),
               reads=[n["on"]], writes=[f"agin{par}"], dma_chan=f"dma_oout{par}")

    TPB = dm.TBLK // 128

    def emit_tblock(tb, PP):
        parity = tb % 2
        woff = parity * dm.TBLK
        for j in range(TPB):
            i = tb * TPB + j
            xtb, xnb = xt[i % NXT], xn[i % NXT]
            xtn, xnn = f"xt{i % NXT}", f"xn{i % NXT}"
            PP.dma("sync", lambda e, i=i, xtb=xtb: e.dma_start(out=xtb[:, :],
                                                              in_=xb[i * 128:(i + 1) * 128, :]),
                  writes=[xtn])
            emit_ln_tile(PP, t, lnp, "a", i % 2, xtn, xtb[:, :], xnn, xnb[:, :], dm)
            PP.emit("vector", lambda e, xnb=xnb, i=i: e.scalar_tensor_tensor(
                out=dead_v[i % 2][:, :], in0=xnb[:, :], scalar=1.0, in1=wr128[:, :],
                op0=ALU.bypass, op1=ALU.mult, accum_out=scores[:, i:i + 1]),
                reads=[xnn, "wr128"], writes=[f"dead_v{i % 2}", "scores"])
            tcol = woff + j * 128
            hb, lb = xnhi[i % 2], xnlo[i % 2]
            hn, ln_ = f"xnhi{i % 2}", f"xnlo{i % 2}"
            PP.emit("vector", lambda e, hb=hb, xnb=xnb: e.tensor_copy(hb[:, :], xnb[:, :]),
                    reads=[xnn], writes=[hn])
            PP.emit("vector", lambda e, hb=hb, lb=lb, xnb=xnb: e.tensor_sub(lb[:, :], xnb[:, :],
                                                                            hb[:, :]),
                    reads=[xnn, hn], writes=[ln_])
            PP.dma("sync", lambda e, i=i, hb=hb: e.dma_start(
                out=xndh[i * 128:(i + 1) * 128, :], in_=hb[:, :]),
                reads=[hn], writes=["xndh"], dma_chan=f"dma_xndh{i % 2}")
            PP.dma("sync", lambda e, i=i, lb=lb: e.dma_start(
                out=xndl[i * 128:(i + 1) * 128, :], in_=lb[:, :]),
                reads=[ln_], writes=["xndl"], dma_chan=f"dma_xndl{i % 2}")
        for dc in range(dm.DC):
            PP.dma("sync", lambda e, dc=dc, tb=tb, woff=woff: e.dma_start_transpose(
                out=xnTh[:, dc * WIN + woff:dc * WIN + woff + dm.TBLK],
                in_=xndh[tb * dm.TBLK:(tb + 1) * dm.TBLK, dc * 128:(dc + 1) * 128]),
                reads=["xndh"], writes=["xnTh"], dma_chan=f"dma_xTh{parity}")
            PP.dma("sync", lambda e, dc=dc, tb=tb, woff=woff: e.dma_start_transpose(
                out=xnTl[:, dc * WIN + woff:dc * WIN + woff + dm.TBLK],
                in_=xndl[tb * dm.TBLK:(tb + 1) * dm.TBLK, dc * 128:(dc + 1) * 128]),
                reads=["xndl"], writes=["xnTl"], dma_chan=f"dma_xTl{parity}")
        jblocks = [(0, 128, "qf", 0), (128, 256, "qf", 1), (256, 384, "kf", 0), (384, 512, "kf", 1),
                   (512, 640, "vf", 0), (640, 768, "vf", 1), (768, 832, "z", 0)]
        qwin = (tb * dm.TBLK // QW) % 2
        qoff = (tb * dm.TBLK) % QW
        for jbi, (j0, j1, kind, half) in enumerate(jblocks):
            pjb = ps_proj
            pjn = "b0"
            passes = [(wch, "wch", xnTh, "xnTh"), (wcl, "wcl", xnTh, "xnTh"),
                      (wch, "wch", xnTl, "xnTl")]
            for pi, (wtl, wtn, xv, xn_) in enumerate(passes):
                for dc in range(dm.DC):
                    PP.emit("tensor", lambda e, dc=dc, j0=j0, j1=j1, woff=woff, pjb=pjb,
                            wtl=wtl, xv=xv, pi=pi: e.matmul(
                        pjb[0:(j1 - j0), 0:dm.TBLK], lhsT=wtl[dc][:, j0:j1],
                        rhs=xv[:, dc * WIN + woff:dc * WIN + woff + dm.TBLK],
                        start=(pi == 0 and dc == 0), stop=(pi == 2 and dc == dm.DC - 1)),
                        reads=[f"{wtn}{dc}", xn_], writes=[pjn])
            if kind == "z":
                PP.emit("scalar", lambda e, pjb=pjb: e.activation(out=useg[:, 0:dm.TBLK],
                                                                 in_=pjb[0:M, 0:dm.TBLK],
                                                                 func=AF.Exp, scale=-1.0),
                       reads=[pjn], writes=["useg"])
                PP.emit("scalar", lambda e: e.activation(out=Lseg[:, 0:dm.TBLK], in_=useg[:, 0:dm.TBLK],
                                                        func=AF.Ln, bias=t["cb_one"][0:M, 0:1]),
                       reads=["useg", "cb_one"], writes=["Lseg"])
                ini = 0.0 if tb == 0 else Bg[:, tb * dm.TBLK - 1:tb * dm.TBLK]
                PP.emit("vector", lambda e, tb=tb, ini=ini: e.tensor_tensor_scan(
                    out=Bg[:, tb * dm.TBLK:(tb + 1) * dm.TBLK], data0=Lseg[:, 0:dm.TBLK],
                    data1=Lseg[:, 0:dm.TBLK], initial=ini, op0=ALU.add, op1=ALU.bypass),
                    reads=["Lseg", "Bg"], writes=["Bg"])
            else:
                dst = {"qf": qf, "kf": kf, "vf": vf}[kind][qwin]
                dn = f"{kind}{qwin}"
                PP.emit("scalar", lambda e, dst=dst, half=half, qoff=qoff, pjb=pjb: e.copy(
                    dst[:, half * QW + qoff:half * QW + qoff + dm.TBLK], pjb[:, 0:dm.TBLK]),
                       reads=[pjn], writes=[dn])

    TB_PER_W = QW // dm.TBLK
    NW = dm.T // QW

    def emit_p1_window(w, PP):
        for tbl in range(TB_PER_W):
            emit_tblock(w * TB_PER_W + tbl, PP)
        qwin = w % 2
        for buf, nm in ((qf[qwin], f"qf{qwin}"), (kf[qwin], f"kf{qwin}")):
            PP.emit("scalar", lambda e, buf=buf: e.activation(out=buf[:, :], in_=buf[:, :],
                                                              func=AF.Silu),
                    reads=[nm], writes=[nm])

    def emit_chunk_group(w, PP):
        g0, g1 = w * (QW // C), (w + 1) * (QW // C)
        for cc in range(g0, g1):
            A, B = Rec(), Rec()
            emit_chunk_head(cc, A)
            emit_chunk_state(cc, A)
            if cc >= 1:
                emit_chunk_tail(cc - 1, B)
            replay_interleaved(PP, [A, B])

    for w in range(NW):
        R1, R2 = Rec(), Rec()
        emit_p1_window(w, R1)
        if w >= 1:
            emit_chunk_group(w - 1, R2)
        replay_interleaved(P, [R1, R2])
    emit_chunk_group(NW - 1, P)
    emit_chunk_tail(dm.NCH - 1, P)

    KSEL = float(3 * dm.T // 4)
    P.emit("gpsimd", lambda e: e.memset(lo_t[:, :], -64.0), writes=["lo_t"])
    P.emit("gpsimd", lambda e: e.memset(hi_t[:, :], 64.0), writes=["hi_t"])
    P.emit("gpsimd", lambda e: e.memset(onescol[:, :], 1.0), writes=["onescol"])
    for _it in range(46):
        P.emit("vector", lambda e: e.tensor_scalar(out=mid_t[:, :], in0=lo_t[:, :], scalar1=hi_t[0:1, 0:1],
                                                   scalar2=0.5, op0=ALU.add, op1=ALU.mult),
               reads=["lo_t", "hi_t"], writes=["mid_t"])
        P.emit("tensor", lambda e: e.matmul(pb[6][:, 0:1], lhsT=ones128[:, :], rhs=mid_t[0:1, 0:1],
                                            start=True, stop=True),
               reads=["ones128", "mid_t"], writes=["b6"])
        P.emit("vector", lambda e: e.tensor_copy(mid128[:, :], pb[6][:, 0:1]), reads=["b6"],
               writes=["mid128"])
        P.emit("vector", lambda e: e.tensor_scalar(out=dead64[:, :], in0=scores[:, :],
                                                   scalar1=mid128[:, 0:1],
                                                   scalar2=None, op0=ALU.is_gt, op1=ALU.add,
                                                   accum_out=cnt_c[:, :]),
               reads=["scores", "mid128"], writes=["dead64", "cnt_c"])
        P.emit("tensor", lambda e: e.matmul(pb[6][0:1, 1:2], lhsT=cnt_c[:, 0:1], rhs=onescol[:, 0:1],
                                            start=True, stop=True),
               reads=["cnt_c", "onescol"], writes=["b6"])
        P.emit("vector", lambda e: e.tensor_copy(tot_t[:, :], pb[6][0:1, 1:2]), reads=["b6"],
               writes=["tot_t"])
        P.emit("vector", lambda e: e.tensor_scalar(out=sel_t[:, :], in0=tot_t[:, :], scalar1=KSEL - 0.5,
                                                   scalar2=None, op0=ALU.is_ge),
               reads=["tot_t"], writes=["sel_t"])
        P.emit("vector", lambda e: e.tensor_scalar(out=isel_t[:, :], in0=sel_t[:, :], scalar1=-1.0,
                                                   scalar2=1.0, op0=ALU.mult, op1=ALU.add),
               reads=["sel_t"], writes=["isel_t"])
        P.emit("vector", lambda e: e.tensor_sub(d1_t[:, :], mid_t[:, :], lo_t[:, :]),
               reads=["mid_t", "lo_t"], writes=["d1_t"])
        P.emit("vector", lambda e: e.tensor_sub(d2_t[:, :], mid_t[:, :], hi_t[:, :]),
               reads=["mid_t", "hi_t"], writes=["d2_t"])
        P.emit("vector", lambda e: e.scalar_tensor_tensor(out=lo_t[:, :], in0=d1_t[:, :],
                                                          scalar=sel_t[0:1, 0:1], in1=lo_t[:, :],
                                                          op0=ALU.mult, op1=ALU.add),
               reads=["d1_t", "sel_t", "lo_t"], writes=["lo_t"])
        P.emit("vector", lambda e: e.scalar_tensor_tensor(out=hi_t[:, :], in0=d2_t[:, :],
                                                          scalar=isel_t[0:1, 0:1], in1=hi_t[:, :],
                                                          op0=ALU.mult, op1=ALU.add),
               reads=["d2_t", "isel_t", "hi_t"], writes=["hi_t"])
    P.emit("tensor", lambda e: e.matmul(pb[6][:, 2:3], lhsT=ones128[:, :], rhs=lo_t[0:1, 0:1],
                                        start=True, stop=True),
           reads=["ones128", "lo_t"], writes=["b6"])
    P.emit("vector", lambda e: e.tensor_copy(thr128[:, :], pb[6][:, 2:3]), reads=["b6"],
           writes=["thr128"])

    rg = [[0, 1, 2, 3], [4, 5, 6, 7]]
    P.emit("gpsimd", lambda e: e.collective_compute("AllGather", ALU.bypass, ins=[agin[:, :]],
                                                    outs=[agout[:, :]], replica_groups=rg),
           reads=["agin0", "agin1"], writes=["agout"])

    all_bufs = list(set(list(P.last_write.keys()) + list(P.readers.keys())))
    P.emit("gpsimd", lambda e: e.memset(fencet[:, :], 0.0), writes=all_bufs + ["fence"])
    stack1.close()

    wbx = [sb(f"wb{dc}", [128, D]) for dc in range(dm.DC)]
    wox = [sb(f"wo{dc}", [128, D]) for dc in range(dm.DC)]
    xt2 = [sb(f"p5xt{i}", [128, D]) for i in range(2)]
    ot2 = [sb(f"p5ot{i}", [128, D]) for i in range(2)]
    xn2 = [sb(f"p5xn{i}", [128, D]) for i in range(2)]
    lnp2 = make_ln_scratch(sb, "b", 2)
    xnT2 = [sb(f"p5xnT{i}", [128, D]) for i in range(2)]
    oT2 = [sb(f"p5oT{i}", [128, D]) for i in range(2)]

    def mk_l2s(p):
        d = {}
        for nm in ("gel", "asil", "gpass", "dtmp", "g_t", "t3", "yt"):
            d[nm] = sb(f"l2{p}_{nm}", [128, D])
        d["dmask"] = sb(f"l2{p}_dmask", [128, 1])
        d["sc_c"] = sb(f"l2{p}_sc_c", [128, 1])
        return d
    l2s = [mk_l2s(0), mk_l2s(1)]

    ixt = sb("ixt", [128, 4 * NC5], mybir.dt.int32)
    wr128b = sb("wr128b", [128, D])
    dead5 = sb("dead5", [128, D])
    P.dma("sync", lambda e: e.dma_start(out=ixt[:, :], in_=agidx[:, :]),
          reads=["fence"], writes=["ixt"])
    P.dma("sync", lambda e: e.dma_start(out=wr128b[:, :], in_=wr.to_broadcast([128, D])),
          reads=["fence"], writes=["wr128b"])
    for dc in range(dm.DC):
        P.dma("sync", lambda e, dc=dc: e.dma_start(out=wbx[dc][:, :], in_=wbd[dc * 128:(dc + 1) * 128, :]),
              reads=["fence"], writes=[f"wb{dc}"])
        P.dma("sync", lambda e, dc=dc: e.dma_start(out=wox[dc][:, :], in_=wod[dc * 128:(dc + 1) * 128, :]),
              reads=["fence"], writes=[f"wo{dc}"])


    psT5 = (pb[4], pb[5])

    def transpose_into5(PP, src_name, srct, dst_name, dst):
        for g in range(dm.DC // 4):
            bank = "b4" if g % 2 == 0 else "b5"
            ps = psT5[0] if g % 2 == 0 else psT5[1]
            for jj in range(4):
                dc = g * 4 + jj
                PP.emit("tensor", lambda e, dc=dc, jj=jj, ps=ps, srct=srct: e.transpose(
                    ps[:, jj * 128:(jj + 1) * 128], srct[:, dc * 128:(dc + 1) * 128], t["ident"][:, :]),
                    reads=[src_name, "ident"], writes=[bank])
            eng = "vector" if g % 2 == 0 else "scalar"

            def _evac2(e, g=g, ps=ps, dst=dst):
                d = dst[:, g * 512:(g + 1) * 512]
                return e.tensor_copy(d, ps[:, :]) if hasattr(e, "tensor_copy") else e.copy(d, ps[:, :])
            PP.emit(eng, _evac2, reads=[bank], writes=[dst_name])

    def emit_p5_front(ci, PP):
        par = ci % 2
        s = l2s[par]
        n = {k: f"l2{par}_{k}" for k in s}
        xtn, otn, xnn = f"p5xt{par}", f"p5ot{par}", f"p5xn{par}"
        xnTn, oTn = f"p5xnT{par}", f"p5oT{par}"
        PP.dma("sync", lambda e, ci=ci, par=par: e.dma_start(out=xt2[par][:, :],
                                                             in_=xs[ci * 128:(ci + 1) * 128, :]),
               reads=["fence"], writes=[xtn])
        for j in range(4):
            PP.dma("gpsimd", lambda e, ci=ci, par=par, j=j: e.indirect_dma_start(
                out=ot2[par][:, j * V:(j + 1) * V], out_offset=None, in_=agout[:, :],
                in_offset=IndirectOffsetOnAxis(ixt[:, ci * 4 + j:ci * 4 + j + 1], 0)),
                reads=["agout", "ixt"], writes=[otn], dma_chan=f"dma_ot{par}")
        emit_ln_tile(PP, t, lnp2, "b", par, xtn, xt2[par][:, :], xnn, xn2[par][:, :], dm)
        PP.emit("vector", lambda e, par=par, s=s: e.scalar_tensor_tensor(
            out=dead5[:, :], in0=xn2[par][:, :], scalar=1.0, in1=wr128b[:, :],
            op0=ALU.bypass, op1=ALU.mult, accum_out=s["sc_c"][:, :]),
            reads=[xnn, "wr128b"], writes=["dead5", n["sc_c"]])
        PP.emit("scalar", lambda e, par=par, s=s: e.activation(out=s["gpass"][:, :], in_=xn2[par][:, :],
                                                               func=AF.Silu),
                reads=[xnn], writes=[n["gpass"]])
        transpose_into5(PP, xnn, xn2[par], xnTn, xnT2[par])
        transpose_into5(PP, otn, ot2[par], oTn, oT2[par])
        for eb in range(2):
            for dc in range(dm.DC):
                PP.emit("tensor", lambda e, dc=dc, eb=eb, par=par: e.matmul(
                    (pb[0] if eb == 0 else pb[1])[:, :], lhsT=xnT2[par][:, dc * 128:(dc + 1) * 128],
                    rhs=wbx[dc][:, eb * 512:(eb + 1) * 512],
                    start=(dc == 0), stop=(dc == dm.DC - 1)),
                    reads=[xnTn, f"wb{dc}"], writes=[f"b{eb}"])
        PP.emit("scalar", lambda e, s=s: e.activation(out=s["gel"][:, 0:512], in_=pb[0][:, :],
                                                      func=AF.Gelu_apprx_tanh),
                reads=["b0"], writes=[n["gel"]])
        PP.emit("scalar", lambda e, s=s: e.activation(out=s["gel"][:, 512:1024], in_=pb[1][:, :],
                                                      func=AF.Gelu_apprx_tanh),
                reads=["b1"], writes=[n["gel"]])

    def emit_p5_back(ci, PP):
        par = ci % 2
        s = l2s[par]
        n = {k: f"l2{par}_{k}" for k in s}
        xtn, oTn, xnTn = f"p5xt{par}", f"p5oT{par}", f"p5xnT{par}"
        PP.emit("scalar", lambda e, s=s: e.activation(out=s["asil"][:, :], in_=s["gel"][:, :],
                                                      func=AF.Silu),
                reads=[n["gel"]], writes=[n["asil"]])
        PP.emit("vector", lambda e, ci=ci, s=s: e.tensor_scalar(
            out=s["dmask"][:, :], in0=s["sc_c"][:, :],
            scalar1=thr128[:, 0:1], scalar2=None, op0=ALU.is_gt),
            reads=[n["sc_c"], "thr128"], writes=[n["dmask"]])
        PP.emit("vector", lambda e, s=s: e.tensor_sub(s["dtmp"][:, :], s["asil"][:, :], s["gpass"][:, :]),
                reads=[n["asil"], n["gpass"]], writes=[n["dtmp"]])
        PP.emit("vector", lambda e, s=s: e.scalar_tensor_tensor(out=s["g_t"][:, :], in0=s["dtmp"][:, :],
                                                                scalar=s["dmask"][:, 0:1],
                                                                in1=s["gpass"][:, :],
                                                                op0=ALU.mult, op1=ALU.add),
                reads=[n["dtmp"], n["dmask"], n["gpass"]], writes=[n["g_t"]])
        for eb in range(2):
            for dc in range(dm.DC):
                PP.emit("tensor", lambda e, dc=dc, eb=eb, par=par: e.matmul(
                    (pb[2] if eb == 0 else pb[3])[:, :], lhsT=oT2[par][:, dc * 128:(dc + 1) * 128],
                    rhs=wox[dc][:, eb * 512:(eb + 1) * 512],
                    start=(dc == 0), stop=(dc == dm.DC - 1)),
                    reads=[oTn, f"wo{dc}"], writes=[f"b{eb + 2}"])
        for eb in range(2):
            PP.emit("vector", lambda e, eb=eb, s=s: e.tensor_mul(
                s["t3"][:, eb * 512:(eb + 1) * 512], s["g_t"][:, eb * 512:(eb + 1) * 512],
                (pb[2] if eb == 0 else pb[3])[:, :]),
                reads=[n["g_t"], f"b{eb + 2}"], writes=[n["t3"]])
        PP.emit("vector", lambda e, par=par, s=s: e.tensor_add(s["yt"][:, :], s["t3"][:, :],
                                                               xt2[par][:, :]),
                reads=[n["t3"], xtn], writes=[n["yt"]])
        PP.dma("sync", lambda e, ci=ci, s=s: e.dma_start(out=y[ci * 128:(ci + 1) * 128, :],
                                                         in_=s["yt"][:, :]),
               reads=[n["yt"]], writes=["y"], dma_chan=f"dma_y{par}")

    for ci in range(NC5):
        A, B = Rec(), Rec()
        emit_p5_front(ci, A)
        if ci >= 1:
            emit_p5_back(ci - 1, B)
        replay_interleaved(P, [A, B])
    emit_p5_back(NC5 - 1, P)

    with nc.Block() as block:
        P.finalize_and_run(stack, block)
    return nc, stack


def kernel(L0_attn_x, L0_attn_Wq, L0_attn_Wk, L0_attn_Wv, L0_attn_Wf, L0_attn_g_norm_w,
           L0_attn_Wo, L0_ffn_w_router, L0_ffn_w_block):
    dm = DIMS
    x = np.ascontiguousarray(np.asarray(L0_attn_x, dtype=np.float32))
    Wq = np.asarray(L0_attn_Wq, dtype=np.float32)
    Wk = np.asarray(L0_attn_Wk, dtype=np.float32)
    Wv = np.asarray(L0_attn_Wv, dtype=np.float32)
    Wf = np.asarray(L0_attn_Wf, dtype=np.float32)
    gnw = np.asarray(L0_attn_g_norm_w, dtype=np.float32)
    Wo = np.ascontiguousarray(np.asarray(L0_attn_Wo, dtype=np.float32))
    wrr = np.asarray(L0_ffn_w_router, dtype=np.float32)
    wb = np.ascontiguousarray(np.asarray(L0_ffn_w_block, dtype=np.float32))
    B = x.shape[0]

    nc, stack = build_fused(dm)
    in_maps = []
    for c in range(8):
        b, h = c // 4, c % 4
        wcat = np.concatenate([
            Wq[:, h * dm.KD:(h + 1) * dm.KD],
            Wk[:, h * dm.KD:(h + 1) * dm.KD],
            Wv[:, h * dm.V:(h + 1) * dm.V],
            Wf[:, h * dm.M:(h + 1) * dm.M],
        ], axis=1)
        in_maps.append({
            "xb": np.ascontiguousarray(x[b]),
            "wcat": np.ascontiguousarray(wcat),
            "wr": np.ascontiguousarray(wrr[None, :]),
            "gnwd": np.ascontiguousarray(gnw[None, :]),
            "wbd": wb,
            "wod": Wo,
            "xs": np.ascontiguousarray(x[b, h * dm.SL:(h + 1) * dm.SL]),
            "agidx": np.ascontiguousarray(
                (np.arange(128, dtype=np.int32)[:, None]
                 + (np.arange(dm.SL // 128 * 4, dtype=np.int32)[None, :] // 4) * 128
                 + (np.arange(dm.SL // 128 * 4, dtype=np.int32)[None, :] % 4) * dm.T
                 + h * dm.SL).astype(np.int32)),
        })
    res = run_bass_kernel_spmd(nc, in_maps, core_ids=list(range(8)))
    yv = np.empty((B, dm.T, dm.D), dtype=np.float32)
    for c in range(8):
        b, h = c // 4, c % 4
        yv[b, h * dm.SL:(h + 1) * dm.SL] = np.asarray(res.results[c]["y"])
    stack.close()
    return yv


# version 18
# speedup vs baseline: 1.7241x; 1.4832x over previous
import numpy as np
import concourse.bass as bass
from concourse.bass_core_rust import IndirectOffsetOnAxis
import concourse.mybir as mybir
from concourse.bass_utils import run_bass_kernel_spmd

F32 = mybir.dt.float32
AF = mybir.ActivationFunctionType
ALU = mybir.AluOpType
AX = mybir.AxisListType

EPS = 1e-5


class Dims:
    def __init__(self, T=8192, D=1024, KD=256, M=64, V=256, TBLK=512):
        self.T = T
        self.D = D
        self.KD = KD
        self.M = M
        self.V = V
        self.C = 128
        self.TBLK = TBLK
        self.NTB = T // TBLK
        self.NCH = T // self.C
        self.DC = D // 128
        self.JCOLS = 3 * KD + M
        self.NSL = 4
        self.SL = T // 4


DIMS = Dims()


def _agidx(dm, h):
    QTRC = max(1, dm.NCH // 16) * 128
    r = np.arange(128, dtype=np.int64)[:, None]
    cols = np.arange(dm.SL // 128 * 4, dtype=np.int64)[None, :]
    ci, j = cols // 4, cols % 4
    tt = h * dm.SL + ci * 128 + r
    rows = (tt // QTRC) * (4 * QTRC) + j * QTRC + tt % QTRC
    return np.ascontiguousarray(rows.astype(np.int32))


ENGINES = ("sync", "scalar", "tensor", "vector", "gpsimd")


class _Op:
    __slots__ = ("engine", "build", "deps", "is_dma", "chan", "needs_inc", "seq", "tag", "no_inc",
                 "inc_amt")

    def __init__(self, engine, build, is_dma, chan, tag, no_inc=False, inc_amt=None):
        self.engine = engine
        self.build = build
        self.deps = []
        self.is_dma = is_dma
        self.chan = chan
        self.needs_inc = False
        self.seq = None
        self.tag = tag
        self.no_inc = no_inc
        self.inc_amt = inc_amt if inc_amt is not None else (16 if is_dma else 1)


class Prog:
    def __init__(self, nc):
        self.nc = nc
        self.ops = []
        self.last_write = {}
        self.readers = {}

    def emit(self, engine, build, reads=(), writes=(), is_dma=False, tag="", dma_chan=None,
             no_inc=False, inc_amt=None):
        if is_dma:
            chan = dma_chan or ("dma_" + (writes[0] if writes else reads[0]))
        else:
            chan = engine
        op = _Op(engine, build, is_dma, chan, tag, no_inc, inc_amt)
        deps = []

        def need(other):
            return not (other.is_dma and is_dma and other.chan == chan)

        for b in reads:
            w = self.last_write.get(b)
            if w is not None and need(w):
                deps.append(w)
        for b in writes:
            w = self.last_write.get(b)
            if w is not None and need(w):
                deps.append(w)
            for r in self.readers.get(b, ()):
                if need(r):
                    deps.append(r)
        op.deps = deps
        for b in reads:
            self.readers.setdefault(b, []).append(op)
        for b in writes:
            self.last_write[b] = op
            self.readers[b] = []
        self.ops.append(op)
        return op

    def dma(self, engine, build, reads=(), writes=(), tag="", dma_chan=None):
        return self.emit(engine, build, reads, writes, is_dma=True, tag=tag, dma_chan=dma_chan)

    def finalize_and_run(self, stack, block):
        nc = self.nc
        for op in self.ops:
            for d in op.deps:
                if not d.no_inc:
                    d.needs_inc = True
        for op in self.ops:
            if op.is_dma:
                op.needs_inc = True
        counts = {}
        for op in self.ops:
            if op.needs_inc:
                counts[op.chan] = counts.get(op.chan, 0) + op.inc_amt
                op.seq = counts[op.chan]
        chans = {c: stack.enter_context(nc.semaphore(("ch_" + c)[:48])) for c in counts}
        dma_finals = {c: v for c, v in counts.items() if c.startswith("dma_")}

        def mk(engine_name):
            def body(eng):
                seen = {}
                for op in self.ops:
                    if op.engine != engine_name:
                        continue
                    w = {}
                    for d in op.deps:
                        if d.seq is None:
                            continue
                        w[d.chan] = max(w.get(d.chan, 0), d.seq)
                    for c, v in w.items():
                        if v > seen.get(c, 0):
                            eng.wait_ge(chans[c], v)
                            seen[c] = v
                    inst = op.build(eng)
                    if op.needs_inc and not op.no_inc:
                        inst.then_inc(chans[op.chan], op.inc_amt)
                if engine_name == "sync":
                    for c, v in dma_finals.items():
                        eng.wait_ge(chans[c], v)
            return body

        block.sync(mk("sync"))
        block.scalar(mk("scalar"))
        block.tensor(mk("tensor"))
        block.vector(mk("vector"))
        block.gpsimd(mk("gpsimd"))


class Rec:

    def __init__(self):
        self.items = []

    def emit(self, *a, **kw):
        self.items.append(("emit", a, kw))

    def dma(self, *a, **kw):
        self.items.append(("dma", a, kw))


def replay_interleaved(P, recs):
    recs = [r for r in recs if r.items]
    k = len(recs)
    idx = [0] * k
    while True:
        best, bf = -1, 2.0
        for i in range(k):
            if idx[i] < len(recs[i].items):
                f = idx[i] / len(recs[i].items)
                if f < bf:
                    bf, best = f, i
        if best < 0:
            break
        kind, a, kw = recs[best].items[idx[best]]
        idx[best] += 1
        (P.emit if kind == "emit" else P.dma)(*a, **kw)


def emit_consts(P, t):
    P.emit("gpsimd", lambda e: e.memset(t["ident"][:, :], 0.0), writes=["ident"])
    P.emit("gpsimd", lambda e: e.affine_select(out=t["ident"][:, :], in_=t["ident"][:, :],
                                               compare_op=ALU.not_equal, fill=1.0, base=0,
                                               pattern=[[-1, 128]], channel_multiplier=1),
           reads=["ident"], writes=["ident"])
    P.emit("gpsimd", lambda e: e.memset(t["cb_one"][:, :], 1.0), writes=["cb_one"])
    P.emit("gpsimd", lambda e: e.memset(t["cb_eps"][:, :], EPS), writes=["cb_eps"])


def make_ln_scratch(sb, tag, n=2):
    out = []
    for p in range(n):
        d = {}
        d["bno"] = sb(f"bno{tag}{p}", [128, 12])
        d["bna"] = sb(f"bna{tag}{p}", [128, 2])
        d["lnv_c"] = sb(f"lnvc{tag}{p}", [128, 1])
        d["r_c"] = sb(f"rc{tag}{p}", [128, 1])
        d["nmr_c"] = sb(f"nmrc{tag}{p}", [128, 1])
        out.append(d)
    return out


def emit_ln_tile(P, t, lnp, tag, par, xt_name, xt_ap, xn_name, xn_ap, dm):
    s = lnp[par]
    n = {k: f"ln{tag}{par}_{k}" for k in s}
    bno, bna = s["bno"], s["bna"]
    P.emit("vector", lambda e: e.bn_stats(bno[:, 0:6], xt_ap[:, 0:512]),
           reads=[xt_name], writes=[n["bno"]])
    P.emit("vector", lambda e: e.bn_stats(bno[:, 6:12], xt_ap[:, 512:1024]),
           reads=[xt_name], writes=[n["bno"]])
    P.emit("vector", lambda e: e.bn_aggr(bna[:, :], bno[:, :]), reads=[n["bno"]], writes=[n["bna"]])
    P.emit("scalar", lambda e: e.activation(out=s["lnv_c"][:, :], in_=bna[:, 1:2], func=AF.Ln,
                                            bias=t["cb_eps"][:, 0:1]),
           reads=[n["bna"], "cb_eps"], writes=[n["lnv_c"]])
    P.emit("scalar", lambda e: e.activation(out=s["r_c"][:, :], in_=s["lnv_c"][:, :], func=AF.Exp,
                                            scale=-0.5),
           reads=[n["lnv_c"]], writes=[n["r_c"]])
    P.emit("vector", lambda e: e.scalar_tensor_tensor(out=s["nmr_c"][:, :], in0=bna[:, 0:1],
                                                      scalar=-1.0, in1=s["r_c"][:, :],
                                                      op0=ALU.mult, op1=ALU.mult),
           reads=[n["bna"], n["r_c"]], writes=[n["nmr_c"]])
    P.emit("scalar", lambda e: e.activation(out=xn_ap, in_=xt_ap, func=AF.Identity,
                                            scale=s["r_c"][:, 0:1], bias=s["nmr_c"][:, 0:1]),
           reads=[xt_name, n["r_c"], n["nmr_c"]], writes=[xn_name])


def build_fused(dm):
    from contextlib import ExitStack
    nc = bass.Bass("TRN2", target_bir_lowering=False)
    stack = ExitStack()
    T, D, KD, M, V, C = dm.T, dm.D, dm.KD, dm.M, dm.V, dm.C
    SL = dm.SL
    NC5 = SL // 128

    xb = nc.dram_tensor("xb", [T, D], F32, kind="ExternalInput").ap()
    wcat = nc.dram_tensor("wcat", [D, dm.JCOLS], F32, kind="ExternalInput").ap()
    wr = nc.dram_tensor("wr", [1, D], F32, kind="ExternalInput").ap()
    gnwd = nc.dram_tensor("gnwd", [1, V], F32, kind="ExternalInput").ap()
    wbd = nc.dram_tensor("wbd", [D, D], F32, kind="ExternalInput").ap()
    wod = nc.dram_tensor("wod", [D, D], F32, kind="ExternalInput").ap()
    xs = nc.dram_tensor("xs", [SL, D], F32, kind="ExternalInput").ap()
    agidx = nc.dram_tensor("agidx", [128, 4 * NC5], mybir.dt.int32, kind="ExternalInput").ap()
    y = nc.dram_tensor("y", [SL, D], F32, kind="ExternalOutput").ap()
    agin = nc.dram_tensor("agin", [T, V], F32, kind="Internal").ap()
    agout = nc.dram_tensor("agout", [4 * T, V], F32, kind="Internal").ap()

    sb = lambda name, shape, dt=F32: stack.enter_context(nc.sbuf_tensor(name, shape, dt))
    thr128 = sb("thr128", [128, 1])
    lo_t = sb("lo_t", [1, 1]); hi_t = sb("hi_t", [1, 1]); mid_t = sb("mid_t", [1, 1])
    sel_t = sb("sel_t", [1, 1]); isel_t = sb("isel_t", [1, 1])
    d1_t = sb("d1_t", [1, 1]); d2_t = sb("d2_t", [1, 1]); tot_t = sb("tot_t", [1, 1])
    mid128 = sb("mid128", [128, 1]); cnt_c = sb("cnt_c", [128, 1])
    onescol = sb("onescol", [128, 1]); dead64 = sb("dead64", [128, T // 128])
    fencet = sb("fencet", [1, 1])
    t = {}
    t["ident"] = sb("ident", [128, 128])
    t["cb_one"] = sb("cb_one", [128, 1])
    t["cb_eps"] = sb("cb_eps", [128, 1])
    mask01 = sb("mask01", [128, 128])
    ones128 = sb("ones128", [1, 128])
    zcol = sb("zcol", [M, 1])
    gnw = sb("gnw", [128, V])
    scores = sb("scores", [128, T // 128])
    stack1 = ExitStack()
    sb1 = lambda name, shape, dt=F32: stack1.enter_context(nc.sbuf_tensor(name, shape, dt))
    wr128 = sb1("wr128", [128, D])
    NXT = 2
    xt = [sb1(f"xt{i}", [128, D]) for i in range(NXT)]
    xn = [sb1(f"xn{i}", [128, D]) for i in range(NXT)]
    lnp = make_ln_scratch(sb1, "a", 2)
    dead_v = [sb1("dead_v0", [128, D])] * 2
    WIN = 2 * dm.TBLK
    xnT = sb1("xnT", [128, dm.DC * WIN])
    wc = [sb1(f"wc{dc}", [128, dm.JCOLS]) for dc in range(dm.DC)]
    QW = min(1024, T)
    qf = [sb1(f"qf{i}", [128, 2 * QW]) for i in range(2)]
    kf = [sb1(f"kf{i}", [128, 2 * QW]) for i in range(2)]
    vf = [sb1(f"vf{i}", [128, 2 * QW]) for i in range(2)]
    useg = sb1("useg", [M, dm.TBLK])
    Lseg = sb1("Lseg", [M, dm.TBLK])
    Bg = sb1("Bg", [M, T])
    hkr = [[sb1(f"hk{r}_0", [128, M]), sb1(f"hk{r}_1", [128, M])] for r in range(3)]
    hvr = [sb1(f"hv{r}", [M, V]) for r in range(3)]

    def mk_cs(p):
        d = {}
        for nm, shape in (("bias_c", [M, 1]), ("en", [M, C]), ("stT", [M, C]), ("brel", [M, C]),
                          ("eend", [M, 1]), ("eer", [1, M]), ("eer128", [128, M]), ("E", [128, M]),
                          ("stil", [128, M]), ("atm", [128, 128]), ("logits", [128, M]),
                          ("rmax", [128, 1]), ("nb16", [128, 1]), ("p_sb", [128, M]),
                          ("den", [128, 1]), ("rden", [128, 1]), ("pe", [128, M]),
                          ("pet", [M, 128]), ("gtm", [128, 128]), ("vt", [128, V]),
                          ("kt", [128, KD]), ("o32", [128, V]), ("osq", [128, V]),
                          ("ssq", [128, 1]), ("lnv", [128, 1]), ("rstd", [128, 1]),
                          ("on", [128, V]), ("t1a", [128, M]), ("t1b", [128, M]), ("t2", [M, V])):
            d[nm] = sb1(f"cs{p}_{nm}", shape)
        return d
    cs = [mk_cs(0), mk_cs(1)]

    pb = [stack.enter_context(nc.psum_tensor(f"pb{i}", [128, 512], F32)) for i in range(8)]
    ps_proj = pb[0]
    psT = (pb[0], pb[0])

    def mk_ps(base):
        bsm, bvk, bop = base
        return {
            "at": pb[bsm][:, 0:128], "gt": pb[bsm][:, 128:256],
            "logit": pb[bsm][:, 256:320], "brelT": pb[bsm][:, 320:384],
            "eer": pb[bsm][0:1, 384:448], "eerep": pb[bsm][:, 448:512],
            "stil": pb[bop][:, 384:448],
            "vt": pb[bvk][:, 0:256], "kt": pb[bvk][:, 256:512],
            "o": pb[bop][:, 0:256], "pet": pb[bop][0:M, 256:384],
            "bsm": f"b{bsm}", "bvk": f"b{bvk}", "bop": f"b{bop}",
        }
    psc = [mk_ps((1, 2, 3)), mk_ps((4, 5, 6))]
    ps_hk = pb[7][:, 0:128]
    ps_hv = pb[7][0:M, 128:384]

    P = Prog(nc)

    emit_consts(P, t)
    P.emit("gpsimd", lambda e: e.memset(mask01[:, :], 1.0), writes=["mask01"])
    P.emit("gpsimd", lambda e: e.affine_select(out=mask01[:, :], in_=mask01[:, :], compare_op=ALU.is_ge,
                                               fill=0.0, base=0, pattern=[[1, 128]], channel_multiplier=-1),
           reads=["mask01"], writes=["mask01"])
    P.emit("gpsimd", lambda e: e.memset(ones128[:, :], 1.0), writes=["ones128"])
    P.emit("gpsimd", lambda e: e.memset(zcol[:, :], 0.0), writes=["zcol"])
    P.emit("gpsimd", lambda e: e.memset(hkr[2][0][:, :], 0.0), writes=["hk2_0"])
    P.emit("gpsimd", lambda e: e.memset(hkr[2][1][:, :], 0.0), writes=["hk2_1"])
    P.emit("gpsimd", lambda e: e.memset(hvr[2][:, :], 0.0), writes=["hv2"])
    P.dma("sync", lambda e: e.dma_start(out=gnw[:, :], in_=gnwd.to_broadcast([128, V])), writes=["gnw"])
    P.dma("sync", lambda e: e.dma_start(out=wr128[:, :], in_=wr.to_broadcast([128, D])), writes=["wr128"])
    for dc in range(dm.DC):
        P.dma("sync", lambda e, dc=dc: e.dma_start(out=wc[dc][:, :], in_=wcat[dc * 128:(dc + 1) * 128, :]),
              writes=[f"wc{dc}"])

    def chunk_env(c):
        par = c % 2
        s = cs[par]
        ps = psc[par]
        n = {k: f"c{par}_{k}" for k in s}
        qwin = (c * C // QW) % 2
        return par, s, ps, n, qwin

    def emit_chunk_head(c, PP):
        c0 = c * C
        par, s, ps, n, qwin = chunk_env(c)
        bsm, bvk, bop = ps["bsm"], ps["bvk"], ps["bop"]
        off = (c * C) % QW
        Bc = Bg[:, c0 - 1:c0] if c > 0 else zcol[:, :]
        bsrc = ["Bg"] if c > 0 else ["zcol"]
        qfp, kfp, vfp = qf[qwin], kf[qwin], vf[qwin]
        qn, kn, vn = f"qf{qwin}", f"kf{qwin}", f"vf{qwin}"
        hki = hkr[(c - 1) % 3]
        hkin = [f"hk{(c - 1) % 3}_0", f"hk{(c - 1) % 3}_1"]
        PP.emit("vector", lambda e: e.tensor_scalar_mul(s["bias_c"][:, :], Bc, -1.0 / 8.0),
               reads=bsrc, writes=[n["bias_c"]])
        PP.emit("scalar", lambda e: e.activation(out=s["en"][:, :], in_=Bg[:, c0:c0 + C], func=AF.Exp,
                                                scale=1.0 / 8.0, bias=s["bias_c"][:, 0:1]),
               reads=["Bg", n["bias_c"]], writes=[n["en"]])
        PP.emit("vector", lambda e: e.tensor_scalar_add(s["stT"][:, 0:1], s["en"][:, 0:1], -1.0),
               reads=[n["en"]], writes=[n["stT"]])
        PP.emit("vector", lambda e: e.tensor_sub(s["stT"][:, 1:C], s["en"][:, 1:C], s["en"][:, 0:C - 1]),
               reads=[n["en"]], writes=[n["stT"]])
        PP.emit("vector", lambda e: e.tensor_scalar(out=s["brel"][:, :], in0=Bg[:, c0:c0 + C],
                                                   scalar1=Bc, scalar2=None, op0=ALU.subtract),
               reads=["Bg"] + bsrc, writes=[n["brel"]])
        PP.emit("scalar", lambda e: e.activation(out=s["eend"][:, :], in_=s["brel"][:, C - 1:C],
                                                func=AF.Exp, scale=-1.0 / 8.0),
               reads=[n["brel"]], writes=[n["eend"]])
        PP.emit("tensor", lambda e: e.transpose(ps["brelT"], s["brel"][:, :], t["ident"][0:M, 0:M]),
               reads=[n["brel"], "ident"], writes=[bsm])
        PP.emit("scalar", lambda e: e.activation(out=s["E"][:, :], in_=ps["brelT"], func=AF.Exp,
                                                scale=-1.0 / 8.0),
               reads=[bsm], writes=[n["E"]])
        PP.emit("tensor", lambda e: e.transpose(ps["eer"], s["eend"][:, :], t["ident"][0:M, 0:M]),
               reads=[n["eend"], "ident"], writes=[bsm])
        PP.emit("vector", lambda e: e.tensor_copy(s["eer"][:, :], ps["eer"]), reads=[bsm], writes=[n["eer"]])
        PP.emit("tensor", lambda e: e.matmul(ps["eerep"], lhsT=ones128[:, :], rhs=s["eer"][:, :],
                                            start=True, stop=True),
               reads=["ones128", n["eer"]], writes=[bsm])
        PP.emit("vector", lambda e: e.tensor_copy(s["eer128"][:, :], ps["eerep"]), reads=[bsm],
               writes=[n["eer128"]])
        PP.emit("tensor", lambda e: e.transpose(ps["stil"], s["stT"][:, :], t["ident"][0:M, 0:M]),
               reads=[n["stT"], "ident"], writes=[bop])
        PP.emit("vector", lambda e: e.tensor_copy(s["stil"][:, :], ps["stil"]), reads=[bop],
               writes=[n["stil"]])
        PP.emit("tensor", lambda e: e.matmul(ps["at"], lhsT=kfp[:, off:off + C], rhs=qfp[:, off:off + C],
                                            start=True, stop=False),
               reads=[kn, qn], writes=[bsm])
        PP.emit("tensor", lambda e: e.matmul(ps["at"], lhsT=kfp[:, QW + off:QW + off + C],
                                            rhs=qfp[:, QW + off:QW + off + C], start=False, stop=True),
               reads=[kn, qn], writes=[bsm])
        PP.emit("vector", lambda e: e.scalar_tensor_tensor(out=s["atm"][:, :], in0=ps["at"], scalar=1.0,
                                                          in1=mask01[:, :], op0=ALU.bypass, op1=ALU.mult),
               reads=[bsm, "mask01"], writes=[n["atm"]])
        PP.emit("tensor", lambda e: e.matmul(ps["logit"], lhsT=s["atm"][:, :], rhs=s["stil"][:, :],
                                            start=True, stop=False),
               reads=[n["atm"], n["stil"]], writes=[bsm])
        PP.emit("tensor", lambda e: e.matmul(ps["logit"], lhsT=qfp[:, off:off + C], rhs=hki[0][:, :],
                                            start=False, stop=False),
               reads=[qn, hkin[0]], writes=[bsm])
        PP.emit("tensor", lambda e: e.matmul(ps["logit"], lhsT=qfp[:, QW + off:QW + off + C],
                                            rhs=hki[1][:, :], start=False, stop=True),
               reads=[qn, hkin[1]], writes=[bsm])
        PP.emit("tensor", lambda e: e.transpose(ps["vt"][:, 0:128], vfp[:, off:off + C], t["ident"][:, :]),
               reads=[vn, "ident"], writes=[bvk])
        PP.emit("tensor", lambda e: e.transpose(ps["vt"][:, 128:256], vfp[:, QW + off:QW + off + C],
                                               t["ident"][:, :]),
               reads=[vn, "ident"], writes=[bvk])
        PP.emit("scalar", lambda e: e.copy(s["vt"][:, :], ps["vt"]), reads=[bvk], writes=[n["vt"]])
        PP.emit("tensor", lambda e: e.transpose(ps["kt"][:, 0:128], kfp[:, off:off + C], t["ident"][:, :]),
               reads=[kn, "ident"], writes=[bvk])
        PP.emit("tensor", lambda e: e.transpose(ps["kt"][:, 128:256], kfp[:, QW + off:QW + off + C],
                                               t["ident"][:, :]),
               reads=[kn, "ident"], writes=[bvk])
        PP.emit("scalar", lambda e: e.copy(s["kt"][:, :], ps["kt"]), reads=[bvk], writes=[n["kt"]])

    def emit_chunk_state(c, PP):
        par, s, ps, n, qwin = chunk_env(c)
        hki = hkr[(c - 1) % 3]
        hkin = [f"hk{(c - 1) % 3}_0", f"hk{(c - 1) % 3}_1"]
        hko = hkr[c % 3]
        hkon = [f"hk{c % 3}_0", f"hk{c % 3}_1"]
        hvi, hvin = hvr[(c - 1) % 3], f"hv{(c - 1) % 3}"
        hvo, hvon = hvr[c % 3], f"hv{c % 3}"
        PP.emit("tensor", lambda e: e.matmul(ps_hk[:, 0:64], lhsT=s["kt"][:, 0:128], rhs=s["stil"][:, :],
                                            start=True, stop=True),
               reads=[n["kt"], n["stil"]], writes=["b7"])
        PP.emit("tensor", lambda e: e.matmul(ps_hk[:, 64:128], lhsT=s["kt"][:, 128:256],
                                            rhs=s["stil"][:, :], start=True, stop=True),
               reads=[n["kt"], n["stil"]], writes=["b7"])
        for i, tn in ((0, "t1a"), (1, "t1b")):
            PP.emit("vector", lambda e, i=i, tn=tn: e.tensor_add(s[tn][:, :],
                                                                ps_hk[:, 64 * i:64 * (i + 1)],
                                                                hki[i][:, :]),
                   reads=["b7", hkin[i]], writes=[n[tn]])
            PP.emit("vector", lambda e, i=i, tn=tn: e.tensor_mul(hko[i][:, :], s[tn][:, :],
                                                                s["eer128"][:, :]),
                   reads=[n[tn], n["eer128"]], writes=[hkon[i]])
        PP.emit("tensor", lambda e: e.matmul(ps_hv, lhsT=s["stil"][:, :], rhs=s["vt"][:, :],
                                            start=True, stop=True),
               reads=[n["stil"], n["vt"]], writes=["b7"])
        PP.emit("vector", lambda e: e.tensor_add(s["t2"][:, :], ps_hv, hvi[:, :]),
               reads=["b7", hvin], writes=[n["t2"]])
        PP.emit("vector", lambda e: e.tensor_scalar(out=hvo[:, :], in0=s["t2"][:, :],
                                                   scalar1=s["eend"][:, 0:1], scalar2=None, op0=ALU.mult),
               reads=[n["t2"], n["eend"]], writes=[hvon])

    def emit_chunk_tail(c, PP):
        c0 = c * C
        par, s, ps, n, qwin = chunk_env(c)
        bsm, bvk, bop = ps["bsm"], ps["bvk"], ps["bop"]
        hvi, hvin = hvr[(c - 1) % 3], f"hv{(c - 1) % 3}"
        PP.emit("vector", lambda e: e.tensor_tensor(out=s["logits"][:, :], in0=ps["logit"],
                                                   in1=s["E"][:, :], op=ALU.mult),
               reads=[bsm, n["E"]], writes=[n["logits"]])
        PP.emit("vector", lambda e: e.reduce_max(s["rmax"][:, :], s["logits"][:, :], axis=AX.X),
               reads=[n["logits"]], writes=[n["rmax"]])
        PP.emit("vector", lambda e: e.tensor_scalar_mul(s["nb16"][:, :], s["rmax"][:, :], -1.0 / 16.0),
               reads=[n["rmax"]], writes=[n["nb16"]])
        PP.emit("scalar", lambda e: e.activation(out=s["p_sb"][:, :], in_=s["logits"][:, :], func=AF.Exp,
                                                scale=1.0 / 16.0, bias=s["nb16"][:, 0:1],
                                                accum_out=s["den"][:, :]),
               reads=[n["logits"], n["nb16"]], writes=[n["p_sb"], n["den"]])
        PP.emit("vector", lambda e: e.reciprocal(s["rden"][:, :], s["den"][:, :]),
               reads=[n["den"]], writes=[n["rden"]])
        PP.emit("vector", lambda e: e.scalar_tensor_tensor(out=s["pe"][:, :], in0=s["p_sb"][:, :],
                                                          scalar=s["rden"][:, 0:1], in1=s["E"][:, :],
                                                          op0=ALU.mult, op1=ALU.mult),
               reads=[n["p_sb"], n["rden"], n["E"]], writes=[n["pe"]])
        PP.emit("tensor", lambda e: e.transpose(ps["pet"], s["pe"][:, :], t["ident"][:, :]),
               reads=[n["pe"], "ident"], writes=[bop])
        PP.emit("vector", lambda e: e.tensor_copy(s["pet"][:, :], ps["pet"]), reads=[bop], writes=[n["pet"]])
        PP.emit("tensor", lambda e: e.matmul(ps["gt"], lhsT=s["stT"][:, :], rhs=s["pet"][:, :],
                                            start=True, stop=True),
               reads=[n["stT"], n["pet"]], writes=[bsm])
        PP.emit("vector", lambda e: e.scalar_tensor_tensor(out=s["gtm"][:, :], in0=ps["gt"], scalar=1.0,
                                                          in1=mask01[:, :], op0=ALU.bypass, op1=ALU.mult),
               reads=[bsm, "mask01"], writes=[n["gtm"]])
        PP.emit("tensor", lambda e: e.matmul(ps["o"], lhsT=s["gtm"][:, :], rhs=s["vt"][:, :],
                                            start=True, stop=False),
               reads=[n["gtm"], n["vt"]], writes=[bop])
        PP.emit("tensor", lambda e: e.matmul(ps["o"], lhsT=s["pet"][:, :], rhs=hvi[:, :],
                                            start=False, stop=True),
               reads=[n["pet"], hvin], writes=[bop])
        PP.emit("vector", lambda e: e.tensor_copy(s["o32"][:, :], ps["o"]), reads=[bop], writes=[n["o32"]])
        PP.emit("scalar", lambda e: e.activation(out=s["osq"][:, :], in_=s["o32"][:, :], func=AF.Square,
                                                accum_out=s["ssq"][:, :]),
               reads=[n["o32"]], writes=[n["osq"], n["ssq"]])
        PP.emit("scalar", lambda e: e.activation(out=s["lnv"][:, :], in_=s["ssq"][:, :], func=AF.Ln,
                                                scale=1.0 / V, bias=t["cb_eps"][:, 0:1]),
               reads=[n["ssq"], "cb_eps"], writes=[n["lnv"]])
        PP.emit("scalar", lambda e: e.activation(out=s["rstd"][:, :], in_=s["lnv"][:, :], func=AF.Exp,
                                                scale=-0.5),
               reads=[n["lnv"]], writes=[n["rstd"]])
        PP.emit("vector", lambda e: e.scalar_tensor_tensor(out=s["on"][:, :], in0=s["o32"][:, :],
                                                          scalar=s["rstd"][:, 0:1], in1=gnw[:, :],
                                                          op0=ALU.mult, op1=ALU.mult),
               reads=[n["o32"], n["rstd"], "gnw"], writes=[n["on"]])
        PP.dma("sync", lambda e: e.dma_start(out=agin[c0:c0 + C, :], in_=s["on"][:, :]),
               reads=[n["on"]], writes=[f"agin_q{c // max(1, dm.NCH // 16)}_{par}"],
               dma_chan=f"dma_oout{par}")

    TPB = dm.TBLK // 128

    def emit_tblock(tb, PP):
        parity = tb % 2
        woff = parity * dm.TBLK
        for j in range(TPB):
            i = tb * TPB + j
            xtb, xnb = xt[i % NXT], xn[i % NXT]
            xtn, xnn = f"xt{i % NXT}", f"xn{i % NXT}"
            PP.dma("sync", lambda e, i=i, xtb=xtb: e.dma_start(out=xtb[:, :],
                                                              in_=xb[i * 128:(i + 1) * 128, :]),
                  writes=[xtn])
            emit_ln_tile(PP, t, lnp, "a", i % 2, xtn, xtb[:, :], xnn, xnb[:, :], dm)
            PP.emit("vector", lambda e, xnb=xnb, i=i: e.scalar_tensor_tensor(
                out=dead_v[i % 2][:, :], in0=xnb[:, :], scalar=1.0, in1=wr128[:, :],
                op0=ALU.bypass, op1=ALU.mult, accum_out=scores[:, i:i + 1]),
                reads=[xnn, "wr128"], writes=[f"dead_v{i % 2}", "scores"])
            tcol = woff + j * 128
            for g in range(dm.DC // 4):
                bank = "b0"
                ps = psT[0]
                for jj in range(4):
                    dc = g * 4 + jj
                    PP.emit("tensor", lambda e, dc=dc, jj=jj, ps=ps, xnb=xnb: e.transpose(
                        ps[:, jj * 128:(jj + 1) * 128], xnb[:, dc * 128:(dc + 1) * 128],
                        t["ident"][:, :]),
                        reads=[xnn, "ident"], writes=[bank])
                eng = "vector" if g % 2 == 0 else "scalar"

                def _evac(e, g=g, ps=ps, tcol=tcol):
                    dst = xnT[:, :].rearrange("p (d w) -> p d w", d=dm.DC)[:, g * 4:g * 4 + 4,
                                                                          tcol:tcol + 128]
                    src = ps[:, :].rearrange("p (c w) -> p c w", c=4)
                    return e.tensor_copy(dst, src) if hasattr(e, "tensor_copy") else e.copy(dst, src)
                PP.emit(eng, _evac, reads=[bank], writes=["xnT"])
        jblocks = [(0, 128, "qf", 0), (128, 256, "qf", 1), (256, 384, "kf", 0), (384, 512, "kf", 1),
                   (512, 640, "vf", 0), (640, 768, "vf", 1), (768, 832, "z", 0)]
        qwin = (tb * dm.TBLK // QW) % 2
        qoff = (tb * dm.TBLK) % QW
        for jbi, (j0, j1, kind, half) in enumerate(jblocks):
            pjb = ps_proj
            pjn = "b0"
            for dc in range(dm.DC):
                PP.emit("tensor", lambda e, dc=dc, j0=j0, j1=j1, jw=jw_ if False else (j1 - j0), woff=woff,
                       pjb=pjb: e.matmul(
                    pjb[0:(j1 - j0), 0:dm.TBLK], lhsT=wc[dc][:, j0:j1],
                    rhs=xnT[:, :].rearrange("p (d w) -> p d w", d=dm.DC)[:, dc, woff:woff + dm.TBLK],
                    start=(dc == 0), stop=(dc == dm.DC - 1)),
                    reads=[f"wc{dc}", "xnT"], writes=[pjn])
            if kind == "z":
                PP.emit("scalar", lambda e, pjb=pjb: e.activation(out=useg[:, 0:dm.TBLK],
                                                                 in_=pjb[0:M, 0:dm.TBLK],
                                                                 func=AF.Exp, scale=-1.0),
                       reads=[pjn], writes=["useg"])
                PP.emit("scalar", lambda e: e.activation(out=Lseg[:, 0:dm.TBLK], in_=useg[:, 0:dm.TBLK],
                                                        func=AF.Ln, bias=t["cb_one"][0:M, 0:1]),
                       reads=["useg", "cb_one"], writes=["Lseg"])
                ini = 0.0 if tb == 0 else Bg[:, tb * dm.TBLK - 1:tb * dm.TBLK]
                PP.emit("vector", lambda e, tb=tb, ini=ini: e.tensor_tensor_scan(
                    out=Bg[:, tb * dm.TBLK:(tb + 1) * dm.TBLK], data0=Lseg[:, 0:dm.TBLK],
                    data1=Lseg[:, 0:dm.TBLK], initial=ini, op0=ALU.add, op1=ALU.bypass),
                    reads=["Lseg", "Bg"], writes=["Bg"])
            else:
                dst = {"qf": qf, "kf": kf, "vf": vf}[kind][qwin]
                dn = f"{kind}{qwin}"
                PP.emit("scalar", lambda e, dst=dst, half=half, qoff=qoff, pjb=pjb: e.copy(
                    dst[:, half * QW + qoff:half * QW + qoff + dm.TBLK], pjb[:, 0:dm.TBLK]),
                       reads=[pjn], writes=[dn])

    TB_PER_W = QW // dm.TBLK
    NW = dm.T // QW

    def emit_p1_window(w, PP):
        for tbl in range(TB_PER_W):
            emit_tblock(w * TB_PER_W + tbl, PP)
        qwin = w % 2
        for buf, nm in ((qf[qwin], f"qf{qwin}"), (kf[qwin], f"kf{qwin}")):
            PP.emit("scalar", lambda e, buf=buf: e.activation(out=buf[:, :], in_=buf[:, :],
                                                              func=AF.Silu),
                    reads=[nm], writes=[nm])

    def emit_chunk_group(w, PP):
        g0, g1 = w * (QW // C), (w + 1) * (QW // C)
        for cc in range(g0, g1):
            A, B = Rec(), Rec()
            emit_chunk_head(cc, A)
            emit_chunk_state(cc, A)
            if cc >= 1:
                emit_chunk_tail(cc - 1, B)
                maybe_emit_ag(cc - 1, B)
            replay_interleaved(PP, [A, B])

    KSEL = float(3 * dm.T // 4)

    def emit_bisect(PP):
        PP.emit("gpsimd", lambda e: e.memset(lo_t[:, :], -64.0), writes=["lo_t"])
        PP.emit("gpsimd", lambda e: e.memset(hi_t[:, :], 64.0), writes=["hi_t"])
        PP.emit("gpsimd", lambda e: e.memset(onescol[:, :], 1.0), writes=["onescol"])
        for _it in range(46):
            PP.emit("vector", lambda e: e.tensor_scalar(out=mid_t[:, :], in0=lo_t[:, :], scalar1=hi_t[0:1, 0:1],
                                                        scalar2=0.5, op0=ALU.add, op1=ALU.mult),
                    reads=["lo_t", "hi_t"], writes=["mid_t"])
            PP.emit("tensor", lambda e: e.matmul(pb[0][:, 0:1], lhsT=ones128[:, :], rhs=mid_t[0:1, 0:1],
                                                 start=True, stop=True),
                    reads=["ones128", "mid_t"], writes=["b0"])
            PP.emit("vector", lambda e: e.tensor_copy(mid128[:, :], pb[0][:, 0:1]), reads=["b0"],
                    writes=["mid128"])
            PP.emit("vector", lambda e: e.tensor_scalar(out=dead64[:, :], in0=scores[:, :],
                                                        scalar1=mid128[:, 0:1],
                                                        scalar2=None, op0=ALU.is_gt, op1=ALU.add,
                                                        accum_out=cnt_c[:, :]),
                    reads=["scores", "mid128"], writes=["dead64", "cnt_c"])
            PP.emit("tensor", lambda e: e.matmul(pb[0][0:1, 1:2], lhsT=cnt_c[:, 0:1], rhs=onescol[:, 0:1],
                                                 start=True, stop=True),
                    reads=["cnt_c", "onescol"], writes=["b0"])
            PP.emit("vector", lambda e: e.tensor_copy(tot_t[:, :], pb[0][0:1, 1:2]), reads=["b0"],
                    writes=["tot_t"])
            PP.emit("vector", lambda e: e.tensor_scalar(out=sel_t[:, :], in0=tot_t[:, :], scalar1=KSEL - 0.5,
                                                        scalar2=None, op0=ALU.is_ge),
                    reads=["tot_t"], writes=["sel_t"])
            PP.emit("vector", lambda e: e.tensor_scalar(out=isel_t[:, :], in0=sel_t[:, :], scalar1=-1.0,
                                                        scalar2=1.0, op0=ALU.mult, op1=ALU.add),
                    reads=["sel_t"], writes=["isel_t"])
            PP.emit("vector", lambda e: e.tensor_sub(d1_t[:, :], mid_t[:, :], lo_t[:, :]),
                    reads=["mid_t", "lo_t"], writes=["d1_t"])
            PP.emit("vector", lambda e: e.tensor_sub(d2_t[:, :], mid_t[:, :], hi_t[:, :]),
                    reads=["mid_t", "hi_t"], writes=["d2_t"])
            PP.emit("vector", lambda e: e.scalar_tensor_tensor(out=lo_t[:, :], in0=d1_t[:, :],
                                                               scalar=sel_t[0:1, 0:1], in1=lo_t[:, :],
                                                               op0=ALU.mult, op1=ALU.add),
                    reads=["d1_t", "sel_t", "lo_t"], writes=["lo_t"])
            PP.emit("vector", lambda e: e.scalar_tensor_tensor(out=hi_t[:, :], in0=d2_t[:, :],
                                                               scalar=isel_t[0:1, 0:1], in1=hi_t[:, :],
                                                               op0=ALU.mult, op1=ALU.add),
                    reads=["d2_t", "isel_t", "hi_t"], writes=["hi_t"])
        PP.emit("tensor", lambda e: e.matmul(pb[0][:, 2:3], lhsT=ones128[:, :], rhs=lo_t[0:1, 0:1],
                                             start=True, stop=True),
                reads=["ones128", "lo_t"], writes=["b0"])
        PP.emit("vector", lambda e: e.tensor_copy(thr128[:, :], pb[0][:, 2:3]), reads=["b0"],
                writes=["thr128"])


    rg = [[0, 1, 2, 3], [4, 5, 6, 7]]
    QTR = max(1, dm.NCH // 16)
    NAG = dm.NCH // QTR

    def maybe_emit_ag(upto_chunk, PP):
        q = (upto_chunk + 1) // QTR - 1
        if q >= 0 and (upto_chunk + 1) % QTR == 0:
            qs = q * QTR * C
            qe = (q + 1) * QTR * C
            PP.emit("gpsimd", lambda e, q=q, qs=qs, qe=qe: e.collective_compute(
                "AllGather", ALU.bypass, ins=[agin[qs:qe, :]],
                outs=[agout.rearrange("(q j r) v -> q (j r) v", q=NAG, j=4)[q, :, :]],
                replica_groups=rg),
                reads=[f"agin_q{q}_0", f"agin_q{q}_1"], writes=["agout"],
                is_dma=True, dma_chan=f"cc_q{q}", inc_amt=1)

    for w in range(NW):
        R1, R2 = Rec(), Rec()
        emit_p1_window(w, R1)
        if w >= 1:
            emit_chunk_group(w - 1, R2)
        replay_interleaved(P, [R1, R2])
    RB = Rec()
    emit_bisect(RB)
    R2 = Rec()
    emit_chunk_group(NW - 1, R2)
    replay_interleaved(P, [R2, RB])
    emit_chunk_tail(dm.NCH - 1, P)
    maybe_emit_ag(dm.NCH - 1, P)

    all_bufs = list(set(list(P.last_write.keys()) + list(P.readers.keys())))
    P.emit("gpsimd", lambda e: e.memset(fencet[:, :], 0.0), writes=all_bufs + ["fence"])
    stack1.close()

    wbx = [sb(f"wb{dc}", [128, D]) for dc in range(dm.DC)]
    wox = [sb(f"wo{dc}", [128, D]) for dc in range(dm.DC)]
    xt2 = [sb(f"p5xt{i}", [128, D]) for i in range(2)]
    ot2 = [sb(f"p5ot{i}", [128, D]) for i in range(2)]
    xn2 = [sb(f"p5xn{i}", [128, D]) for i in range(2)]
    lnp2 = make_ln_scratch(sb, "b", 2)
    xnT2 = [sb(f"p5xnT{i}", [128, D]) for i in range(2)]
    oT2 = [sb(f"p5oT{i}", [128, D]) for i in range(2)]

    def mk_l2s(p):
        d = {}
        for nm in ("gel", "asil", "gpass", "dtmp", "g_t", "t3", "yt"):
            d[nm] = sb(f"l2{p}_{nm}", [128, D])
        d["dmask"] = sb(f"l2{p}_dmask", [128, 1])
        d["sc_c"] = sb(f"l2{p}_sc_c", [128, 1])
        return d
    l2s = [mk_l2s(0), mk_l2s(1)]

    ixt = sb("ixt", [128, 4 * NC5], mybir.dt.int32)
    wr128b = sb("wr128b", [128, D])
    dead5 = sb("dead5", [128, D])
    P.dma("sync", lambda e: e.dma_start(out=ixt[:, :], in_=agidx[:, :]),
          reads=["fence"], writes=["ixt"])
    P.dma("sync", lambda e: e.dma_start(out=wr128b[:, :], in_=wr.to_broadcast([128, D])),
          reads=["fence"], writes=["wr128b"])
    for dc in range(dm.DC):
        P.dma("sync", lambda e, dc=dc: e.dma_start(out=wbx[dc][:, :], in_=wbd[dc * 128:(dc + 1) * 128, :]),
              reads=["fence"], writes=[f"wb{dc}"])
        P.dma("sync", lambda e, dc=dc: e.dma_start(out=wox[dc][:, :], in_=wod[dc * 128:(dc + 1) * 128, :]),
              reads=["fence"], writes=[f"wo{dc}"])


    psT5 = (pb[4], pb[5])

    def transpose_into5(PP, src_name, srct, dst_name, dst):
        for g in range(dm.DC // 4):
            bank = "b4" if g % 2 == 0 else "b5"
            ps = psT5[0] if g % 2 == 0 else psT5[1]
            for jj in range(4):
                dc = g * 4 + jj
                PP.emit("tensor", lambda e, dc=dc, jj=jj, ps=ps, srct=srct: e.transpose(
                    ps[:, jj * 128:(jj + 1) * 128], srct[:, dc * 128:(dc + 1) * 128], t["ident"][:, :]),
                    reads=[src_name, "ident"], writes=[bank])
            eng = "vector" if g % 2 == 0 else "scalar"

            def _evac2(e, g=g, ps=ps, dst=dst):
                d = dst[:, g * 512:(g + 1) * 512]
                return e.tensor_copy(d, ps[:, :]) if hasattr(e, "tensor_copy") else e.copy(d, ps[:, :])
            PP.emit(eng, _evac2, reads=[bank], writes=[dst_name])

    def emit_p5_front(ci, PP):
        par = ci % 2
        s = l2s[par]
        n = {k: f"l2{par}_{k}" for k in s}
        xtn, otn, xnn = f"p5xt{par}", f"p5ot{par}", f"p5xn{par}"
        xnTn, oTn = f"p5xnT{par}", f"p5oT{par}"
        PP.dma("sync", lambda e, ci=ci, par=par: e.dma_start(out=xt2[par][:, :],
                                                             in_=xs[ci * 128:(ci + 1) * 128, :]),
               reads=["fence"], writes=[xtn])
        for j in range(4):
            PP.dma("gpsimd", lambda e, ci=ci, par=par, j=j: e.indirect_dma_start(
                out=ot2[par][:, j * V:(j + 1) * V], out_offset=None, in_=agout[:, :],
                in_offset=IndirectOffsetOnAxis(ixt[:, ci * 4 + j:ci * 4 + j + 1], 0)),
                reads=["agout", "ixt"], writes=[otn], dma_chan=f"dma_ot{par}")
        emit_ln_tile(PP, t, lnp2, "b", par, xtn, xt2[par][:, :], xnn, xn2[par][:, :], dm)
        PP.emit("vector", lambda e, par=par, s=s: e.scalar_tensor_tensor(
            out=dead5[:, :], in0=xn2[par][:, :], scalar=1.0, in1=wr128b[:, :],
            op0=ALU.bypass, op1=ALU.mult, accum_out=s["sc_c"][:, :]),
            reads=[xnn, "wr128b"], writes=["dead5", n["sc_c"]])
        PP.emit("scalar", lambda e, par=par, s=s: e.activation(out=s["gpass"][:, :], in_=xn2[par][:, :],
                                                               func=AF.Silu),
                reads=[xnn], writes=[n["gpass"]])
        transpose_into5(PP, xnn, xn2[par], xnTn, xnT2[par])
        transpose_into5(PP, otn, ot2[par], oTn, oT2[par])
        for eb in range(2):
            for dc in range(dm.DC):
                PP.emit("tensor", lambda e, dc=dc, eb=eb, par=par: e.matmul(
                    (pb[0] if eb == 0 else pb[1])[:, :], lhsT=xnT2[par][:, dc * 128:(dc + 1) * 128],
                    rhs=wbx[dc][:, eb * 512:(eb + 1) * 512],
                    start=(dc == 0), stop=(dc == dm.DC - 1)),
                    reads=[xnTn, f"wb{dc}"], writes=[f"b{eb}"])
        PP.emit("scalar", lambda e, s=s: e.activation(out=s["gel"][:, 0:512], in_=pb[0][:, :],
                                                      func=AF.Gelu_apprx_tanh),
                reads=["b0"], writes=[n["gel"]])
        PP.emit("scalar", lambda e, s=s: e.activation(out=s["gel"][:, 512:1024], in_=pb[1][:, :],
                                                      func=AF.Gelu_apprx_tanh),
                reads=["b1"], writes=[n["gel"]])

    def emit_p5_back(ci, PP):
        par = ci % 2
        s = l2s[par]
        n = {k: f"l2{par}_{k}" for k in s}
        xtn, oTn, xnTn = f"p5xt{par}", f"p5oT{par}", f"p5xnT{par}"
        PP.emit("scalar", lambda e, s=s: e.activation(out=s["asil"][:, :], in_=s["gel"][:, :],
                                                      func=AF.Silu),
                reads=[n["gel"]], writes=[n["asil"]])
        PP.emit("vector", lambda e, ci=ci, s=s: e.tensor_scalar(
            out=s["dmask"][:, :], in0=s["sc_c"][:, :],
            scalar1=thr128[:, 0:1], scalar2=None, op0=ALU.is_gt),
            reads=[n["sc_c"], "thr128"], writes=[n["dmask"]])
        PP.emit("vector", lambda e, s=s: e.tensor_sub(s["dtmp"][:, :], s["asil"][:, :], s["gpass"][:, :]),
                reads=[n["asil"], n["gpass"]], writes=[n["dtmp"]])
        PP.emit("vector", lambda e, s=s: e.scalar_tensor_tensor(out=s["g_t"][:, :], in0=s["dtmp"][:, :],
                                                                scalar=s["dmask"][:, 0:1],
                                                                in1=s["gpass"][:, :],
                                                                op0=ALU.mult, op1=ALU.add),
                reads=[n["dtmp"], n["dmask"], n["gpass"]], writes=[n["g_t"]])
        for eb in range(2):
            for dc in range(dm.DC):
                PP.emit("tensor", lambda e, dc=dc, eb=eb, par=par: e.matmul(
                    (pb[2] if eb == 0 else pb[3])[:, :], lhsT=oT2[par][:, dc * 128:(dc + 1) * 128],
                    rhs=wox[dc][:, eb * 512:(eb + 1) * 512],
                    start=(dc == 0), stop=(dc == dm.DC - 1)),
                    reads=[oTn, f"wo{dc}"], writes=[f"b{eb + 2}"])
        for eb in range(2):
            PP.emit("vector", lambda e, eb=eb, s=s: e.tensor_mul(
                s["t3"][:, eb * 512:(eb + 1) * 512], s["g_t"][:, eb * 512:(eb + 1) * 512],
                (pb[2] if eb == 0 else pb[3])[:, :]),
                reads=[n["g_t"], f"b{eb + 2}"], writes=[n["t3"]])
        PP.emit("vector", lambda e, par=par, s=s: e.tensor_add(s["yt"][:, :], s["t3"][:, :],
                                                               xt2[par][:, :]),
                reads=[n["t3"], xtn], writes=[n["yt"]])
        PP.dma("sync", lambda e, ci=ci, s=s: e.dma_start(out=y[ci * 128:(ci + 1) * 128, :],
                                                         in_=s["yt"][:, :]),
               reads=[n["yt"]], writes=["y"], dma_chan=f"dma_y{par}")

    for ci in range(NC5):
        A, B = Rec(), Rec()
        emit_p5_front(ci, A)
        if ci >= 1:
            emit_p5_back(ci - 1, B)
        replay_interleaved(P, [A, B])
    emit_p5_back(NC5 - 1, P)

    with nc.Block() as block:
        P.finalize_and_run(stack, block)
    return nc, stack


def kernel(L0_attn_x, L0_attn_Wq, L0_attn_Wk, L0_attn_Wv, L0_attn_Wf, L0_attn_g_norm_w,
           L0_attn_Wo, L0_ffn_w_router, L0_ffn_w_block):
    dm = DIMS
    x = np.ascontiguousarray(np.asarray(L0_attn_x, dtype=np.float32))
    Wq = np.asarray(L0_attn_Wq, dtype=np.float32)
    Wk = np.asarray(L0_attn_Wk, dtype=np.float32)
    Wv = np.asarray(L0_attn_Wv, dtype=np.float32)
    Wf = np.asarray(L0_attn_Wf, dtype=np.float32)
    gnw = np.asarray(L0_attn_g_norm_w, dtype=np.float32)
    Wo = np.ascontiguousarray(np.asarray(L0_attn_Wo, dtype=np.float32))
    wrr = np.asarray(L0_ffn_w_router, dtype=np.float32)
    wb = np.ascontiguousarray(np.asarray(L0_ffn_w_block, dtype=np.float32))
    B = x.shape[0]

    nc, stack = build_fused(dm)
    in_maps = []
    for c in range(8):
        b, h = c // 4, c % 4
        wcat = np.concatenate([
            Wq[:, h * dm.KD:(h + 1) * dm.KD],
            Wk[:, h * dm.KD:(h + 1) * dm.KD],
            Wv[:, h * dm.V:(h + 1) * dm.V],
            Wf[:, h * dm.M:(h + 1) * dm.M],
        ], axis=1)
        in_maps.append({
            "xb": np.ascontiguousarray(x[b]),
            "wcat": np.ascontiguousarray(wcat),
            "wr": np.ascontiguousarray(wrr[None, :]),
            "gnwd": np.ascontiguousarray(gnw[None, :]),
            "wbd": wb,
            "wod": Wo,
            "xs": np.ascontiguousarray(x[b, h * dm.SL:(h + 1) * dm.SL]),
            "agidx": _agidx(dm, h),
        })
    res = run_bass_kernel_spmd(nc, in_maps, core_ids=list(range(8)))
    yv = np.empty((B, dm.T, dm.D), dtype=np.float32)
    for c in range(8):
        b, h = c // 4, c % 4
        yv[b, h * dm.SL:(h + 1) * dm.SL] = np.asarray(res.results[c]["y"])
    stack.close()
    return yv
